# version 18
# baseline (speedup 1.0000x reference)
import numpy as np
from contextlib import ExitStack

import concourse.bass as bass
import concourse.mybir as mybir
from concourse.bass_utils import run_bass_kernel_spmd

F32 = mybir.dt.float32
AF = mybir.ActivationFunctionType
ALU = mybir.AluOpType
AX = mybir.AxisListType

N_CORES = 8


def make_cfg(S=2048, D=1024, H=16, F=4096, NB=2):
    dk = D // H
    assert dk == 64 and D % 128 == 0 and S % 512 == 0 and F % 256 == 0 and H % 2 == 0
    return dict(
        S=S, D=D, H=H, F=F, NB=NB, dk=dk,
        DC=D // 128, QT=S // 512, KC=S // 128,
        FB=F // 256, TB=S // 128, HP=H // 2,
        REL=2 * 512 - 1, CLIP=2 * 512 - 2,
    )


def band_map(cfg):
    S, CLIP = cfg["S"], cfg["CLIP"]
    m = {}
    for qt in range(cfg["QT"]):
        for kc in range(cfg["KC"]):
            base = S - 1 + qt * 512 - kc * 128
            if base - 127 < CLIP:
                m[(kc, qt)] = base
    return m, sorted(set(m.values()))


def pack_weight_strips_qkv(W, D):
    DC = D // 128
    HPn = W.shape[1] // 128
    w = W.reshape(DC, 128, HPn, 128)
    return np.ascontiguousarray(w.transpose(2, 1, 0, 3))


def pack_wfc_strips(Wfc, D, F):
    DC = D // 128
    w = Wfc.reshape(F // 256, 2, 128, DC, 128)
    return np.ascontiguousarray(w.transpose(0, 1, 4, 3, 2))


def pack_wpr_strips(Wpr, D, F):
    DC = D // 128
    w = Wpr.reshape(DC, 128, F // 256, 2, 128)
    return np.ascontiguousarray(w.transpose(2, 0, 4, 3, 1))


def pack_bias_cols(b, D):
    return np.ascontiguousarray(b.reshape(D // 128, 128).T)


def pack_bands(rel, cfg):
    _, bases = band_map(cfg)
    CLIP = cfg["CLIP"]
    jj = np.arange(128)[:, None]
    ii = np.arange(512)[None, :]
    out = np.empty((cfg["H"], max(len(bases), 1), 128, 512), dtype=np.float32)
    for bi, base in enumerate(bases):
        idx = np.minimum(base + ii - jj, CLIP)
        out[:, bi] = rel[idx, :].transpose(2, 0, 1)
    return np.ascontiguousarray(out)


class Chan:
    __slots__ = ("name", "sem", "count")

    def __init__(self, name, sem):
        self.name, self.sem, self.count = name, sem, 0


class Prog:
    ENGINES = ("t", "v", "s", "g", "y")

    def __init__(self):
        self.ops = []
        self.wr = {e: set() for e in self.ENGINES}
        self.rd = {e: set() for e in self.ENGINES}
        self.n_drain = 0

    def op(self, eng, fn, waits=(), inc=None, inc_amt=1, reads=(), writes=()):
        for ch, val in waits:
            assert val <= ch.count, (ch.name, val, ch.count)
        if eng in ("v", "s", "g"):
            r, w = set(reads), set(writes)
            if (r | w) & self.wr[eng] or (w & self.rd[eng]):
                self.ops.append((eng, "drain", None))
                self.wr[eng].clear()
                self.rd[eng].clear()
                self.n_drain += 1
            self.rd[eng] |= r
            self.wr[eng] |= w
        self.ops.append((eng, "op", (fn, tuple(waits), inc, inc_amt)))
        if inc is not None:
            inc.count += inc_amt

    def emit(self, eng_key, eng):
        done = {}
        for e, kind, payload in self.ops:
            if e != eng_key:
                continue
            if kind == "drain":
                eng.drain()
                continue
            fn, waits, inc, inc_amt = payload
            for ch, val in waits:
                if val > 0 and done.get(id(ch), -1) < val:
                    eng.wait_ge(ch.sem, val)
                    done[id(ch)] = val
            instr = fn(eng)
            if inc is not None:
                instr.then_inc(inc.sem, inc_amt)


def build_program(cfg):
    S, D, H, F, NB = cfg["S"], cfg["D"], cfg["H"], cfg["F"], cfg["NB"]
    DC, QT, KC, FB, TB, HP = (cfg[k] for k in ("DC", "QT", "KC", "FB", "TB", "HP"))
    PH = cfg.get("ph", 99)
    bmap, bases = band_map(cfg)
    base_idx = {v: i for i, v in enumerate(bases)}
    NBASE = max(len(bases), 1)

    nc = bass.Bass("TRN2", target_bir_lowering=False, debug=False)

    xT = nc.dram_tensor("xT", [D, NB * S], F32, kind="ExternalInput").ap()
    wq = nc.dram_tensor("wq", [HP, 128, DC, 128], F32, kind="ExternalInput").ap()
    wk = nc.dram_tensor("wk", [HP, 128, DC, 128], F32, kind="ExternalInput").ap()
    wv = nc.dram_tensor("wv", [HP, 128, DC, 128], F32, kind="ExternalInput").ap()
    wo = nc.dram_tensor("wo", [DC, 128, DC, 128], F32, kind="ExternalInput").ap()
    wfc = nc.dram_tensor("wfc", [FB, 2, 128, DC, 128], F32, kind="ExternalInput").ap()
    wpr = nc.dram_tensor("wpr", [FB, DC, 128, 2, 128], F32, kind="ExternalInput").ap()
    bqc = nc.dram_tensor("bqc", [128, DC], F32, kind="ExternalInput").ap()
    bkc = nc.dram_tensor("bkc", [128, DC], F32, kind="ExternalInput").ap()
    bvc = nc.dram_tensor("bvc", [128, DC], F32, kind="ExternalInput").ap()
    boc = nc.dram_tensor("boc", [128, DC], F32, kind="ExternalInput").ap()
    crep = nc.dram_tensor("crep", [128, H], F32, kind="ExternalInput").ap()
    bands = nc.dram_tensor("bands", [H, NBASE, 128, 512], F32, kind="ExternalInput").ap()
    ident = nc.dram_tensor("ident", [128, 128], F32, kind="ExternalInput").ap()
    selh = nc.dram_tensor("selh", [33, 128], F32, kind="ExternalInput").ap()
    out = nc.dram_tensor("out", [NB * S, D], F32, kind="ExternalOutput").ap()

    with ExitStack() as ctx:
        def sb(name, shape, dt=F32):
            return ctx.enter_context(nc.sbuf_tensor(name, shape, dt))

        c_ident = sb("c_ident", [128, 128])
        c_identb = sb("c_identb", [128, 128], mybir.dt.bfloat16)
        selbc = sb("selbc", [33, 128])
        c_crep = sb("c_crep", [128, H])
        c_bq = sb("c_bq", [128, DC]); c_bk = sb("c_bk", [128, DC])
        c_bv = sb("c_bv", [128, DC]); c_bo = sb("c_bo", [128, DC])
        onescol = sb("onescol", [128, 1])
        onesrow = sb("onesrow", [1, 128])
        BF16 = mybir.dt.bfloat16
        xhi = sb("xhi", [128, DC, S], BF16)
        xlo = sb("xlo", [128, DC, S], BF16)
        xpc = [sb(f"xpc{i}", [128, 512]) for i in range(2)]
        wbf = [sb(f"wbf{i}", [128, DC * 128]) for i in range(2)]
        whi = [sb(f"whi{i}", [128, DC * 128], BF16) for i in range(2)]
        wlo = [sb(f"wlo{i}", [128, DC * 128], BF16) for i in range(2)]
        bbuf = [sb(f"bbuf{i}", [128, 512]) for i in range(2)]
        bscr = [sb(f"bscr{i}", [128, 512]) for i in range(2)]
        qhi = sb("qhi", [128, S], BF16); qlo = sb("qlo", [128, S], BF16)
        khi = sb("khi", [128, S], BF16); klo = sb("klo", [128, S], BF16)
        vtiles = sb("vtiles", [128, KC, 130])
        eA = [sb(f"eA{i}", [128, 512]) for i in range(2)]
        eB = [sb(f"eB{i}", [128, 512]) for i in range(2)]
        Ur = sb("Ur", [128, DC, S])
        rscr = [sb(f"rscr{i}", [128, 512]) for i in range(2)]
        tm = [sb(f"tm{i}", [128, D]) for i in range(2)]
        ob = [sb("ob0", [128, D])]
        ob = [ob[0], ob[0]]
        nsum = sb("nsum", [128, 1]); negm = sb("negm", [128, 1])
        m2c = sb("m2c", [128, 1]); r2c = sb("r2c", [128, 1])
        sqv = sb("sqv", [128, 1]); ssq = sb("ssq", [128, 1])
        vepsc = sb("vepsc", [128, 1])

        ps = [ctx.enter_context(nc.psum_tensor(f"ps{i}", [128, 512], F32))
              for i in range(8)]

        def chan(name):
            return Chan(name, ctx.enter_context(nc.semaphore(name)))

        cT = chan("cT"); cV = chan("cV"); cS = chan("cS"); cG = chan("cG")
        cDC = chan("cDC")
        cW = [chan("cW0"), chan("cW1")]
        cB = [chan("cB0"), chan("cB1")]
        cX = chan("cX"); cX2 = chan("cX2")
        cO = [chan("cO0"), chan("cO1")]

        P = Prog()

        def mm(e, o, l, r, st, sp, tp=None):
            return e.matmul(o, l, r, start=st, stop=sp, tile_position=tp)

        bank_free = {i: [] for i in range(8)}

        for dst, src in ((c_ident, ident), (c_crep, crep), (c_bq, bqc),
                         (c_bk, bkc), (c_bv, bvc), (c_bo, boc),
                         (selbc, selh)):
            P.op("y", lambda e, d=dst, s=src: e.dma_start(out=d[:], in_=s[:]),
                 inc=cDC, inc_amt=16)
        snap_consts = cDC.count
        P.op("s", lambda e: e.copy(c_identb[:], c_ident[:]),
             waits=[(cDC, snap_consts)], writes=("c_identb",), inc=cS)
        snap_identb = cS.count
        P.op("v", lambda e: e.memset(onesrow[:], 1.0), writes=("onesrow",))
        P.op("v", lambda e: e.memset(onescol[:], 1.0), writes=("onescol",))
        P.op("v", lambda e: e.memset(
            vtiles.ap().rearrange("p k (h f) -> p k h f", f=65)[:, :, :, 64:65], 1.0),
            inc=cV, writes=("vtiles",))

        strip_user = {0: [], 1: []}
        strip_split = {0: [], 1: []}
        sidx = [0]

        def load_strip(src_ap, ncols, split=False):
            par = sidx[0] % 2
            sidx[0] += 1
            P.op("y", lambda e: e.dma_start(out=wbf[par][:, 0:ncols], in_=src_ap),
                 waits=list(strip_user[par]) + list(strip_split[par]),
                 inc=cW[par], inc_amt=16)
            dwsnap = cW[par].count
            if split:
                P.op("s", lambda e: e.copy(whi[par][:, 0:ncols],
                                           wbf[par][:, 0:ncols]),
                     waits=[(cW[par], dwsnap)] + list(strip_user[par]),
                     reads=(f"wbf{par}",), writes=(f"whi{par}",), inc=cS)
                hisnap = cS.count
                P.op("v", lambda e: e.tensor_tensor(
                    wlo[par][:, 0:ncols], wbf[par][:, 0:ncols],
                    whi[par][:, 0:ncols], ALU.subtract),
                    waits=[(cS, hisnap)],
                    reads=(f"wbf{par}", f"whi{par}"), writes=(f"wlo{par}",),
                    inc=cV)
                strip_split[par] = [(cV, cV.count)]
                return par, [(cS, hisnap), (cV, cV.count)]
            strip_split[par] = []
            return par, [(cW[par], dwsnap)]

        xf_T_free = [0]
        xf_V_free = [0]
        g_guard = [0]
        xsplit = [0, 0]
        for b in range(NB):
            bcol = b * S

            xw0 = ([(cT, xf_T_free[0]), (cV, xf_V_free[0])] if b else [])
            cXp = [cX, cX2]
            for i_ in range(DC * QT):
                c, t4 = divmod(i_, QT)
                pi = i_ % 2
                tsl4 = slice(t4 * 512, (t4 + 1) * 512)
                P.op("y", lambda e, c=c, t4=t4, bcol=bcol, pi=pi: e.dma_start(
                    out=xpc[pi][:],
                    in_=xT[c * 128:(c + 1) * 128,
                          bcol + t4 * 512: bcol + (t4 + 1) * 512]),
                    waits=xw0 + ([(cV, xsplit[pi])] if xsplit[pi] else []),
                    inc=cXp[pi], inc_amt=16)
                xw0 = []
                P.op("s", lambda e, c=c, tsl4=tsl4, pi=pi: e.copy(
                    xhi[:, c, tsl4], xpc[pi][:]),
                    waits=[(cXp[pi], cXp[pi].count)] +
                          ([(cT, xf_T_free[0]), (cV, xf_V_free[0])]
                           if (b and i_ == 0) else []),
                    reads=(f"xpc{pi}",), writes=(f"xhi{c}",), inc=cS)
                P.op("v", lambda e, c=c, tsl4=tsl4, pi=pi: e.tensor_tensor(
                    xlo[:, c, tsl4], xpc[pi][:], xhi[:, c, tsl4], ALU.subtract),
                    waits=[(cS, cS.count)],
                    reads=(f"xpc{pi}", f"xhi{c}"), writes=(f"xlo{c}",), inc=cV)
                xsplit[pi] = cV.count
            snap_x = cV.count

            band_guard = [0]
            bscr_guard = [0, 0]
            pending_fin = [None]

            def emit_finish():
                if pending_fin[0] is not None:
                    f = pending_fin[0]
                    pending_fin[0] = None
                    f()
            nHP = HP if PH >= 1 else 0
            if nHP:
                nxt_qk = (load_strip(wq[0].rearrange("p c f -> p (c f)"),
                                     DC * 128, split=True),
                          load_strip(wk[0].rearrange("p c f -> p (c f)"),
                                     DC * 128, split=True))
            for hp in range(nHP):
                hA, hB = 2 * hp, 2 * hp + 1
                pq, pk = nxt_qk

                for pj, ((par, dwsnap), dhi, dlo, bias_sb, dtag) in enumerate(
                        ((pq, qhi, qlo, c_bq, "q"), (pk, khi, klo, c_bk, "k"))):
                    if pj == 1:
                        pv_ = load_strip(wv[hp].rearrange("p c f -> p (c f)"),
                                         DC * 128, split=True)
                    for t in range(QT):
                        bi_ = t % 2
                        bank = ps[bi_]
                        wts = [(cV, snap_x)] + list(dwsnap) + bank_free[bi_]
                        first = True
                        for c in range(DC):
                            for p_ in range(3):
                                lh = (whi, whi, wlo)[p_][par]
                                rh = (xhi, xlo, xhi)[p_]
                                last = (p_ == 2 and c == DC - 1)
                                P.op("t", lambda e, lh=lh, rh=rh, c=c, t=t,
                                     bank=bank, f=first, last=last: mm(
                                         e, bank[:], lh[:, c * 128:(c + 1) * 128],
                                         rh[:, c, t * 512:(t + 1) * 512], f, last),
                                     waits=wts if first else (),
                                     inc=cT if last else None)
                                first = False
                                wts = ()
                        mmsnap = cT.count
                        hs = dhi[:, t * 512:(t + 1) * 512]
                        ls = dlo[:, t * 512:(t + 1) * 512]
                        P.op("v", lambda e, bank=bank, hs=hs, bb=bias_sb, hp=hp:
                             e.tensor_scalar_add(hs, bank[:], bb[:, hp:hp + 1]),
                             waits=[(cT, mmsnap), (cDC, snap_consts)],
                             reads=(f"ps{bi_}",), writes=(f"{dtag}hi{t}",))
                        P.op("v", lambda e, bank=bank, hs=hs, ls=ls, bb=bias_sb,
                             hp=hp: e.scalar_tensor_tensor(
                                 ls, bank[:], bb[:, hp:hp + 1], hs,
                                 ALU.add, ALU.subtract),
                             reads=(f"ps{bi_}", f"{dtag}hi{t}"),
                             writes=(f"{dtag}lo{t}",), inc=cV)
                        bank_free[bi_] = [(cV, cV.count)]
                        if pj == 0 and t == 0:
                            emit_finish()
                    strip_user[par] = [(cT, mmsnap)]

                par, dwsnap = pv_
                for t in range(QT):
                    bi_ = 4 + t % 2
                    bank = ps[bi_]
                    wts = [(cV, snap_x)] + list(dwsnap) + bank_free[bi_]
                    first = True
                    for c in range(DC):
                        for p_ in range(3):
                            lh = (whi, whi, wlo)[p_][par]
                            rh = (xhi, xlo, xhi)[p_]
                            last = (p_ == 2 and c == DC - 1)
                            P.op("t", lambda e, lh=lh, rh=rh, c=c, t=t, bank=bank,
                                 f=first, last=last: mm(
                                     e, bank[:], lh[:, c * 128:(c + 1) * 128],
                                     rh[:, c, t * 512:(t + 1) * 512], f, last),
                                 waits=wts if first else (),
                                 inc=cT if last else None)
                            first = False
                            wts = ()
                    mmsnap = cT.count
                    P.op("v", lambda e, bank=bank, hp=hp: e.tensor_scalar_add(
                        rscr[0][:], bank[:], c_bv[:, hp:hp + 1]),
                        waits=[(cT, mmsnap), (cDC, snap_consts)],
                        reads=(f"ps{bi_}",), writes=("rscr0",), inc=cV)
                    bank_free[bi_] = [(cV, cV.count)]
                    vsn = cV.count
                    for q4 in range(4):
                        kc = t * 4 + q4
                        ti_ = 6 + q4 % 2
                        tb_ = ps[ti_]
                        P.op("t", lambda e, tb_=tb_, q4=q4: e.transpose(
                            tb_[:, 0:128], rscr[0][:, q4 * 128:(q4 + 1) * 128],
                            c_ident[:]),
                            waits=[(cV, vsn), (cDC, snap_consts)] + bank_free[ti_],
                            inc=cT)
                        dst = vtiles.ap().rearrange(
                            "p k (h f) -> p k h f", f=65)[:, kc, :, 0:64]
                        P.op("v", lambda e, tb_=tb_, dst=dst: e.tensor_copy(
                            dst, tb_.ap().rearrange(
                                "p (h f) -> p h f", f=64)[:, 0:2, :]),
                            waits=[(cT, cT.count)], reads=(f"ps{ti_}",),
                            writes=("vtiles",), inc=cV)
                        bank_free[ti_] = [(cV, cV.count)]
                strip_user[par] = [(cT, mmsnap)]
                qkv_ready = cV.count
                if hp + 1 < nHP:
                    nxt_qk = (load_strip(wq[hp + 1].rearrange("p c f -> p (c f)"),
                                         DC * 128, split=True),
                              load_strip(wk[hp + 1].rearrange("p c f -> p (c f)"),
                                         DC * 128, split=True))

                for qt in range(QT if PH >= 2 else 0):
                    qsl = slice(qt * 512, (qt + 1) * 512)
                    s_snaps, exp_snaps, pv_snaps = {}, {}, {}

                    def issue_scores(kc, head=None):
                        par_ = kc % 2
                        heads = ((0, 0), (1, 64)) if head is None else \
                                (((0, 0),) if head == 0 else ((1, 64),))
                        wts0 = []
                        for h_i, r0 in heads:
                            wts0 += bank_free[2 * h_i + par_]
                        if kc < 2:
                            wts0 += [(cV, qkv_ready)]
                        for h_i, r0 in heads:
                            for p_ in range(3):
                                lhK = (khi, khi, klo)[p_]
                                rhQ = (qhi, qlo, qhi)[p_]
                                bi_ = 2 * h_i + par_
                                bank = ps[bi_]
                                P.op("t", lambda e, lhK=lhK, rhQ=rhQ, bank=bank,
                                     r0=r0, kc=kc, qsl=qsl, p_=p_: mm(
                                         e, bank[:],
                                         lhK[r0:r0 + 64, kc * 128:(kc + 1) * 128],
                                         rhQ[r0:r0 + 64, qsl], p_ == 0, p_ == 2,
                                         tp=(r0, 0)),
                                     waits=wts0,
                                     inc=cT if (p_ == 2 and
                                                h_i == heads[-1][0]) else None)
                                wts0 = ()
                        for h_i, _ in heads:
                            s_snaps[(kc, h_i)] = cT.count

                    def prefetch_band(kc):
                        if (kc, qt) not in bmap:
                            return
                        bi = base_idx[bmap[(kc, qt)]]
                        wdma = [(cV, band_guard[0])] if band_guard[0] else []
                        if g_guard[0]:
                            wdma.append((cG, g_guard[0]))
                        for h_i, h in enumerate((hA, hB)):
                            P.op("y", lambda e, h=h, bi=bi, h_i=h_i: e.dma_start(
                                out=bbuf[h_i][:], in_=bands[h, bi]),
                                waits=wdma, inc=cB[h_i], inc_amt=16)
                            wdma = []

                    def issue_exp(kc):
                        par_ = kc % 2
                        banded = (kc, qt) in bmap
                        guard = [(cT, pv_snaps[(kc - 2, 1)])] if kc >= 2 else []
                        for h_i, (h, ebuf) in enumerate(
                                ((hA, eA[par_]), (hB, eB[par_]))):
                            bi_ = 2 * h_i + par_
                            bank = ps[bi_]
                            etag = f"e{'AB'[h_i]}{par_}"
                            if banded:
                                P.op("v", lambda e, bank=bank, h_i=h_i:
                                     e.scalar_tensor_tensor(
                                         bscr[h_i][:], bank[:], 0.125, bbuf[h_i][:],
                                         ALU.mult, ALU.add),
                                     waits=[(cT, s_snaps[(kc, h_i)]),
                                            (cB[h_i], cB[h_i].count),
                                            (cS, bscr_guard[h_i])],
                                     reads=(f"ps{bi_}", f"bbuf{h_i}"),
                                     writes=(f"bscr{h_i}",), inc=cV)
                                band_guard[0] = cV.count
                                bank_free[bi_] = [(cV, cV.count)]
                                P.op("s", lambda e, ebuf=ebuf, h_i=h_i: e.activation(
                                    ebuf[:], bscr[h_i][:], AF.Exp),
                                    waits=[(cV, cV.count)] + guard,
                                    reads=(f"bscr{h_i}",), writes=(etag,), inc=cS)
                                bscr_guard[h_i] = cS.count
                            else:
                                P.op("s", lambda e, bank=bank, ebuf=ebuf, h=h:
                                     e.activation(ebuf[:], bank[:], AF.Exp,
                                                  bias=c_crep[:, h:h + 1],
                                                  scale=0.125),
                                     waits=[(cT, s_snaps[(kc, h_i)]),
                                            (cDC, snap_consts)] + guard,
                                     reads=(f"ps{bi_}",), writes=(etag,), inc=cS)
                                bank_free[bi_] = [(cS, cS.count)]
                            exp_snaps[(kc, h_i)] = cS.count

                    def issue_pv(kc, h_i):
                        par_ = kc % 2
                        ub, ebuf = ((ps[4], eA[par_]), (ps[5], eB[par_]))[h_i]
                        hsl = slice(h_i * 65, h_i * 65 + 65)
                        wts = [(cS, exp_snaps[(kc, h_i)])]
                        if kc == 0:
                            wts += bank_free[4 + h_i]
                        P.op("t", lambda e, ub=ub, ebuf=ebuf, kc=kc, hsl=hsl: mm(
                            e, ub[0:65, :], vtiles[:, kc, hsl], ebuf[:],
                            kc == 0, kc == KC - 1),
                            waits=wts, inc=cT)
                        pv_snaps[(kc, h_i)] = cT.count

                    issue_scores(0)
                    prefetch_band(0)
                    for kc in range(KC):
                        if kc + 1 < KC:
                            issue_scores(kc + 1)
                        if kc == 0:
                            emit_finish()
                        issue_exp(kc)
                        if kc + 1 < KC:
                            prefetch_band(kc + 1)
                        issue_pv(kc, 0)
                        issue_pv(kc, 1)

                    def make_finish(hp=hp, qsl=qsl, pv_last=None):
                        pv_last = pv_snaps[(KC - 1, 1)]

                        def fin():
                            for h_i in range(2):
                                P.op("v", lambda e, ub=ps[4 + h_i], h_i=h_i:
                                     e.reciprocal(
                                         xpc[0][32 * h_i:32 * h_i + 1, :],
                                         ub[64:65, :]),
                                     waits=[(cT, pv_last)],
                                     reads=(f"ps{4 + h_i}",),
                                     writes=("xpc0",), inc=cV)
                            rsnap = cV.count
                            P.op("t", lambda e: mm(
                                e, ps[6][:], selbc[:], xpc[0][0:33, :],
                                True, True),
                                waits=[(cV, rsnap), (cDC, snap_consts)] +
                                      bank_free[6], inc=cT)
                            bcT = cT.count
                            for h_i in range(2):
                                bcs = rscr[h_i][0:64, :]
                                P.op("v", lambda e, bcs=bcs, h_i=h_i:
                                     e.tensor_copy(
                                         bcs, ps[6][h_i * 64:(h_i + 1) * 64, :]),
                                     waits=[(cT, bcT)], reads=("ps6",),
                                     writes=(f"rscr{h_i}",))
                                dst = Ur[:, hp, qsl][h_i * 64:(h_i + 1) * 64, :]
                                P.op("v", lambda e, ub=ps[4 + h_i], bcs=bcs,
                                     dst=dst: e.tensor_tensor(
                                         dst, ub[0:64, :], bcs, ALU.mult),
                                     reads=(f"ps{4 + h_i}", f"rscr{h_i}"),
                                     writes=(f"Ur{hp}",), inc=cV)
                                bank_free[4 + h_i] = [(cV, cV.count)]
                            bank_free[6] = [(cV, cV.count)]
                        return fin

                    pending_fin[0] = make_finish()
            emit_finish()
            snap_attn = cV.count

            res1_t = {}
            for t in range(QT if PH >= 3 else 0):
                tsl = slice(t * 512, (t + 1) * 512)
                lastgrp = 0
                for o in range(DC):
                    par, dwsnap = load_strip(
                        wo[o].rearrange("p c f -> p (c f)"), DC * 128)
                    wts = list(dwsnap) + [(cV, snap_attn)] + bank_free[o]
                    for c in range(DC):
                        P.op("t", lambda e, c=c, o=o, par=par, tsl=tsl: mm(
                            e, ps[o][:], wbf[par][:, c * 128:(c + 1) * 128],
                            Ur[:, c, tsl], c == 0, c == DC - 1),
                            waits=wts if c == 0 else (),
                            inc=cT if c == DC - 1 else None)
                        wts = ()
                    strip_user[par] = [(cT, cT.count)]
                    lastgrp = cT.count
                for o in range(DC):
                    P.op("v", lambda e, o=o, tsl=tsl: e.scalar_tensor_tensor(
                        rscr[o % 2][:], ps[o][:], c_bo[:, o:o + 1], xhi[:, o, tsl],
                        ALU.add, ALU.add),
                        waits=[(cT, lastgrp), (cDC, snap_consts)],
                        reads=(f"ps{o}", f"xhi{o}"), writes=(f"rscr{o % 2}",))
                    P.op("v", lambda e, o=o, tsl=tsl: e.tensor_tensor(
                        Ur[:, o, tsl], rscr[o % 2][:], xlo[:, o, tsl], ALU.add),
                        reads=(f"rscr{o % 2}", f"xlo{o}"), writes=(f"Ur{o}",),
                        inc=cV)
                    bank_free[o] = [(cV, cV.count)]
                res1_t[t] = cV.count
            snap_res1 = cV.count

            nFB = FB if PH >= 5 else 0
            strip_plan = []
            for fb_ in range(nFB):
                strip_plan += [("fc", fb_, o_) for o_ in range(2)]
                strip_plan += [("pr", fb_, o_) for o_ in range(DC)]
            strip_idx = [0]

            def next_strip():
                kind, fb_, o_ = strip_plan[strip_idx[0]]
                strip_idx[0] += 1
                if kind == "fc":
                    return load_strip(wfc[fb_, o_].rearrange("p c f -> p (c f)"),
                                      DC * 128, split=True)
                return load_strip(wpr[fb_, o_].rearrange("p c f -> p (c f)"),
                                  256, split=True)

            pend = [next_strip() if strip_plan else None,
                    next_strip() if len(strip_plan) > 1 else None]

            def pop_strip():
                return pend.pop(0)

            def prefetch_strip():
                pend.append(next_strip()
                            if strip_idx[0] < len(strip_plan) else None)

            snap_x1_t = {}
            QTn = QT if PH >= 4 else 0
            sqk = [0]
            sqmm = {}
            for t in range(QTn):
                tsl = slice(t * 512, (t + 1) * 512)
                wts = [(cV, res1_t.get(t, snap_res1))] + bank_free[t]
                for c in range(DC):
                    P.op("t", lambda e, c=c, tsl=tsl, t=t: mm(
                        e, ps[t][0:1, :], onescol[:], Ur[:, c, tsl],
                        c == 0, c == DC - 1),
                        waits=wts if c == 0 else (),
                        inc=cT if c == DC - 1 else None)
                    wts = ()
                bw = list(bank_free[4 + t])
                for c in range(DC):
                    k = sqk[0]
                    sqk[0] += 1
                    w = [(cV, res1_t.get(t, snap_res1)), (cV, cV.count)]
                    if k >= 2:
                        w.append((cT, sqmm[k - 2]))
                    P.op("s", lambda e, c=c, tsl=tsl, k=k: e.activation(
                        rscr[k % 2][:], Ur[:, c, tsl], AF.Square),
                        waits=w, reads=(f"Ur{c}",), writes=(f"rscr{k % 2}",),
                        inc=cS)
                    P.op("t", lambda e, c=c, t=t, k=k: mm(
                        e, ps[4 + t][0:1, :], onescol[:], rscr[k % 2][:],
                        c == 0, c == DC - 1),
                        waits=[(cS, cS.count)] + bw, inc=cT)
                    bw = []
                    sqmm[k] = cT.count
            if D >= 1024:
                rrows = [tm[0].ap()[0:1, 0:512], tm[0].ap()[0:1, 512:1024],
                         tm[1].ap()[0:1, 0:512], tm[1].ap()[0:1, 512:1024]]
                mrrows = [ob[0].ap()[0:1, 0:512], ob[0].ap()[0:1, 512:1024],
                          xpc[0][0:1, 0:512], xpc[1][0:1, 0:512]]
                rtags = ["tm0a", "tm0b", "tm1a", "tm1b"]
                mtags = ["ob0a", "ob0b", "xpc0", "xpc1"]
            else:
                rrows = [bbuf[0][0:1, 0:512], bbuf[1][0:1, 0:512]] * 2
                mrrows = [xpc[0][0:1, 0:512], xpc[1][0:1, 0:512]] * 2
                rtags = ["bbuf0", "bbuf1"] * 2
                mtags = ["xpc0", "xpc1"] * 2
            rowsnap_t = {}
            for t in range(QTn):
                mrow = bscr[0][0:1, 0:512]; vrow = bscr[1][0:1, 0:512]
                msqr = eA[1][0:1, 0:512]; veps = eA[0][0:1, 0:512]
                sqvr = eB[0][0:1, 0:512]
                rrow, mrrow = rrows[t % 4], mrrows[t % 4]
                rtg, mtg = rtags[t % 4], mtags[t % 4]
                P.op("v", lambda e, mrow=mrow, t=t: e.tensor_scalar_mul(
                    mrow, ps[t][0:1, :], 1.0 / D),
                    waits=[(cT, sqmm[(t + 1) * DC - 1])] +
                          ([(cO[0], cO[0].count), (cO[1], cO[1].count)]
                           if t == 0 else []),
                    reads=(f"ps{t}",), writes=("bscr0",), inc=cV)
                bank_free[t] = [(cV, cV.count)]
                P.op("s", lambda e, mrow=mrow, msqr=msqr: e.activation(
                    msqr, mrow, AF.Square),
                    waits=[(cV, cV.count)], reads=("bscr0",), writes=("eA1",),
                    inc=cS)
                P.op("v", lambda e, vrow=vrow, msqr=msqr, t=t:
                     e.scalar_tensor_tensor(
                         vrow, ps[4 + t][0:1, :], 1.0 / D, msqr,
                         ALU.mult, ALU.subtract),
                     waits=[(cS, cS.count)], reads=(f"ps{4 + t}", "eA1"),
                     writes=("bscr1",), inc=cV)
                bank_free[4 + t] = [(cV, cV.count)]
                P.op("v", lambda e, vrow=vrow, veps=veps: e.tensor_scalar(
                    veps, vrow, 1e-5, None, ALU.add),
                    reads=("bscr1",), writes=("eA0",), inc=cV)
                P.op("s", lambda e, veps=veps, sqvr=sqvr: e.activation(
                    sqvr, veps, AF.Sqrt),
                    waits=[(cV, cV.count)], reads=("eA0",), writes=("eB0",),
                    inc=cS)
                P.op("v", lambda e, rrow=rrow, sqvr=sqvr: e.reciprocal(rrow, sqvr),
                     waits=[(cS, cS.count)], reads=("eB0",), writes=(rtg,))
                P.op("v", lambda e, mrow=mrow, rrow=rrow, mrrow=mrrow:
                     e.tensor_tensor(mrrow, mrow, rrow, ALU.mult),
                     reads=("bscr0", rtg), writes=(mtg,), inc=cV)
                rowsnap_t[t] = cV.count
            bc_t = {}
            for t in range(QTn):
                P.op("t", lambda e, t=t: mm(
                    e, ps[t][:], onesrow[:], rrows[t % 4], True, True),
                    waits=[(cV, rowsnap_t[t])] + bank_free[t], inc=cT)
                P.op("t", lambda e, t=t: mm(
                    e, ps[4 + t][:], onesrow[:], mrrows[t % 4], True, True),
                    waits=bank_free[4 + t], inc=cT)
                bc_t[t] = cT.count
            for t in range(QTn):
                tsl = slice(t * 512, (t + 1) * 512)
                for c in range(DC):
                    P.op("v", lambda e, c=c, tsl=tsl, t=t: e.tensor_tensor(
                        rscr[c % 2][:], Ur[:, c, tsl], ps[t][:], ALU.mult),
                        waits=[(cT, bc_t[t])], reads=(f"Ur{c}", f"ps{t}"),
                        writes=(f"rscr{c % 2}",))
                    P.op("v", lambda e, c=c, tsl=tsl, t=t: e.tensor_tensor(
                        bscr[c % 2][:], rscr[c % 2][:], ps[4 + t][:],
                        ALU.subtract),
                        reads=(f"rscr{c % 2}", f"ps{4 + t}"),
                        writes=(f"bscr{c % 2}",))
                    P.op("v", lambda e, c=c, tsl=tsl: e.tensor_copy(
                        xhi[:, c, tsl], bscr[c % 2][:]),
                        reads=(f"bscr{c % 2}",), writes=(f"xhi{c}",))
                    P.op("v", lambda e, c=c, tsl=tsl: e.tensor_tensor(
                        xlo[:, c, tsl], bscr[c % 2][:], xhi[:, c, tsl],
                        ALU.subtract),
                        reads=(f"bscr{c % 2}", f"xhi{c}"), writes=(f"xlo{c}",),
                        inc=cV)
                bank_free[t] = [(cV, cV.count)]
                bank_free[4 + t] = [(cV, cV.count)]
                snap_x1_t[t] = cV.count
            snap_x1 = cV.count

            h_mm_snap = [0]
            rscr_free = {0: [(cV, snap_x1)], 1: [(cV, snap_x1)]}
            for fb in range(nFB):
                for o in range(2):
                    par, dwsnap = pop_strip()
                    hh = (qhi, qlo)[o]
                    hl = (khi, klo)[o]
                    htg = ("qhi", "qlo")[o]
                    ltg = ("khi", "klo")[o]
                    for t in range(QT):
                        tsl = slice(t * 512, (t + 1) * 512)
                        bi_ = (o * QT + t) % 4
                        bank = ps[bi_]
                        wts = list(dwsnap) + \
                              [(cV, snap_x1_t.get(t, snap_x1))] + bank_free[bi_]
                        if h_mm_snap[0]:
                            wts.append((cT, h_mm_snap[0]))
                        first = True
                        for c in range(DC):
                            for p_ in range(3):
                                lh = (whi, whi, wlo)[p_][par]
                                rh = (xhi, xlo, xhi)[p_]
                                last = (p_ == 2 and c == DC - 1)
                                P.op("t", lambda e, lh=lh, rh=rh, c=c, tsl=tsl,
                                     bank=bank, f=first, last=last: mm(
                                         e, bank[:], lh[:, c * 128:(c + 1) * 128],
                                         rh[:, c, tsl], f, last),
                                     waits=wts if first else (),
                                     inc=cT if last else None)
                                first = False
                                wts = ()
                        strip_user[par] = [(cT, cT.count)]
                        mmsnap = cT.count
                        P.op("s", lambda e, bank=bank, o=o: e.activation(
                            rscr[o][:], bank[:], AF.Relu),
                            waits=[(cT, mmsnap)] + rscr_free[o],
                            reads=(f"ps{bi_}",), writes=(f"rscr{o}",))
                        P.op("s", lambda e, hh=hh, tsl=tsl, o=o: e.activation(
                            hh[:, tsl], rscr[o][:], AF.Square),
                            reads=(f"rscr{o}",), writes=(f"{htg}h{t}",), inc=cS)
                        P.op("v", lambda e, bank=bank, o=o: e.tensor_tensor(
                            bscr[o][:], rscr[o][:], bank[:], ALU.mult),
                            waits=[(cS, cS.count)], reads=(f"rscr{o}", f"ps{bi_}"),
                            writes=(f"bscr{o}",))
                        P.op("v", lambda e, hh=hh, hl=hl, tsl=tsl, o=o:
                             e.tensor_tensor(hl[:, tsl], bscr[o][:], hh[:, tsl],
                                             ALU.subtract),
                             reads=(f"bscr{o}", f"{htg}h{t}"),
                             writes=(f"{ltg}h{t}",), inc=cV)
                        rscr_free[o] = [(cV, cV.count)]
                        bank_free[bi_] = [(cV, cV.count)]
                    prefetch_strip()
                fc_done = cV.count
                for o in range(DC):
                    par, dwsnap = pop_strip()
                    for t in range(QT):
                        tsl = slice(t * 512, (t + 1) * 512)
                        bi_ = 4 + (o * QT + t) % 4
                        bank = ps[bi_]
                        wts = list(dwsnap) + [(cV, fc_done)] + bank_free[bi_]
                        first = True
                        for fc_ in range(2):
                            for p_ in range(3):
                                lh = (whi, whi, wlo)[p_][par]
                                rhh = ((qhi, qlo)[fc_], (khi, klo)[fc_],
                                       (qhi, qlo)[fc_])[p_]
                                last = (p_ == 2 and fc_ == 1)
                                P.op("t", lambda e, lh=lh, rhh=rhh, fc_=fc_,
                                     tsl=tsl, bank=bank, f=first, last=last: mm(
                                         e, bank[:],
                                         lh[:, fc_ * 128:(fc_ + 1) * 128],
                                         rhh[:, tsl], f, last),
                                     waits=wts if first else (),
                                     inc=cT if last else None)
                                first = False
                                wts = ()
                        strip_user[par] = [(cT, cT.count)]
                        mmsnap = cT.count
                        if fb == 0:
                            P.op("v", lambda e, bank=bank, o=o, tsl=tsl:
                                 e.tensor_copy(Ur[:, o, tsl], bank[:]),
                                 waits=[(cT, mmsnap)], reads=(f"ps{bi_}",),
                                 writes=(f"Ur{o}",), inc=cV)
                        else:
                            P.op("v", lambda e, bank=bank, o=o, tsl=tsl:
                                 e.tensor_tensor(Ur[:, o, tsl], Ur[:, o, tsl],
                                                 bank[:], ALU.add),
                                 waits=[(cT, mmsnap)], reads=(f"ps{bi_}", f"Ur{o}"),
                                 writes=(f"Ur{o}",), inc=cV)
                        bank_free[bi_] = [(cV, cV.count)]
                    prefetch_strip()
                h_mm_snap[0] = cT.count
            xf_T_free[0] = cT.count
            snap_y = cV.count

            all_tiles = [(o, t4) for o in range(DC if PH >= 6 else 0)
                         for t4 in range(QT)]
            spk_bank_free = {j: list(bank_free[4 + j]) for j in range(4)}
            spk_mm_done = {}
            use_g = False
            g_tiles = [tl for i, tl in enumerate(all_tiles)
                       if use_g and i % 4 == 3]
            v_tiles = [tl for tl in all_tiles if tl not in g_tiles]

            def spike_chain(engine, pair, bufs, tags, first_waits, idn):
                mems, spks, _accs, mts = bufs
                mtag, stag, _atag, ttag = tags
                ys = [Ur[:, o, t4 * 512:(t4 + 1) * 512] for o, t4 in pair]
                fw = list(first_waits)
                pbank = [ps[4 + j] for j in range(len(pair))]
                acc_done = {}
                for j, (o, t4) in enumerate(pair):
                    P.op(engine, lambda e, y_=ys[j], sp=spks[j]: e.tensor_scalar(
                        sp[:], y_, 1.0, None, ALU.is_gt),
                        waits=fw + list(spk_mm_done.get(j, [])),
                        reads=(f"Ur{o}",), writes=(stag[j],), inc=cV)
                    fw = []
                    P.op("t", lambda e, sp=spks[j], pb=pbank[j]: mm(
                        e, pb[:], idn[:], sp[:], True, False),
                        waits=[(cV, cV.count), (cS, snap_identb)] +
                              spk_bank_free[j], inc=cT)
                    spk_mm_done[j] = [(cT, cT.count)]
                for st in range(2, 6):
                    for j, (o, t4) in enumerate(pair):
                        P.op(engine, lambda e, y_=ys[j], m=mems[j], mt=mts[j],
                             st=st: e.scalar_tensor_tensor(
                                 mt[:], (y_ if st == 2 else m[:]), 0.9,
                                 y_, ALU.mult, ALU.add),
                             reads=((f"Ur{o}",) if st == 2
                                    else (mtag[j], f"Ur{o}")),
                             writes=(ttag[j],))
                    for j, (o, t4) in enumerate(pair):
                        P.op(engine, lambda e, m=mems[j], mt=mts[j], sp=spks[j]:
                             e.tensor_tensor(m[:], mt[:], sp[:], ALU.subtract),
                             reads=(ttag[j], stag[j]), writes=(mtag[j],))
                    for j in range(len(pair)):
                        P.op(engine, lambda e, m=mems[j], sp=spks[j]:
                             e.tensor_scalar(sp[:], m[:], 1.0, None, ALU.is_gt),
                             waits=list(spk_mm_done.get(j, [])),
                             reads=(mtag[j],), writes=(stag[j],), inc=cV)
                        P.op("t", lambda e, sp=spks[j], pb=pbank[j], st=st: mm(
                            e, pb[:], idn[:], sp[:], False, st == 5),
                            waits=[(cV, cV.count)], inc=cT)
                        spk_mm_done[j] = [(cT, cT.count)]
                        if st == 5:
                            acc_done[j] = cT.count
                for j, (o, t4) in enumerate(pair):
                    P.op(engine, lambda e, mt=mts[j], pb=pbank[j], o=o, t4=t4:
                         e.scalar_tensor_tensor(
                             mt[:], pb[:], 0.2, xhi[:, o, t4 * 512:(t4 + 1) * 512],
                             ALU.mult, ALU.add),
                         waits=[(cT, acc_done[j])],
                         reads=(f"ps{4 + j}", f"xhi{o}"), writes=(ttag[j],))
                    P.op(engine, lambda e, y_=ys[j], mt=mts[j], o=o, t4=t4:
                         e.tensor_tensor(
                             y_, mt[:], xlo[:, o, t4 * 512:(t4 + 1) * 512],
                             ALU.add),
                         reads=(ttag[j], f"xlo{o}"), writes=(f"Ur{o}",),
                         inc=(cV if engine == "v" else cG))
                    spk_bank_free[j] = [(cV, cV.count)]
                return ys

            g_spk_done = cG.count
            if D >= 1024:
                tmh = (tm[0].ap()[:, 0:512], tm[0].ap()[:, 512:1024],
                       tm[1].ap()[:, 0:512], tm[1].ap()[:, 512:1024])
                if S >= 2048:
                    spk4 = tuple(qhi.ap()[:, i * 512:(i + 1) * 512]
                                 for i in range(4))
                    spk4_t = ("qhi0", "qhi1", "qhi2", "qhi3")
                    sp_idn = c_identb
                else:
                    spk4 = (bscr[0], bscr[1], bbuf[0], bbuf[1])
                    spk4_t = ("bscr0", "bscr1", "bbuf0", "bbuf1")
                    sp_idn = c_ident
                sp_bufs = ((rscr[0], rscr[1], xpc[0], xpc[1]),
                           spk4,
                           (eA[0], eA[1], eB[0], eB[1]), tmh)
                sp_tags = (("rscr0", "rscr1", "xpc0", "xpc1"),
                           spk4_t,
                           ("eA0", "eA1", "eB0", "eB1"),
                           ("tm0a", "tm0b", "tm1a", "tm1b"))
                W = 4
            else:
                if S >= 1024:
                    spk2 = tuple(qhi.ap()[:, i * 512:(i + 1) * 512]
                                 for i in range(2))
                    spk2_t = ("qhi0", "qhi1")
                    sp_idn = c_identb
                else:
                    spk2 = (bscr[0], bscr[1])
                    spk2_t = ("bscr0", "bscr1")
                    sp_idn = c_ident
                sp_bufs = ((rscr[0], rscr[1]), spk2,
                           (eA[0], eA[1]), (eB[0], eB[1]))
                sp_tags = (("rscr0", "rscr1"), spk2_t,
                           ("eA0", "eA1"), ("eB0", "eB1"))
                W = 2
            vw = [(cV, snap_y)]
            for i in range(0, len(v_tiles), W):
                spike_chain("v", v_tiles[i:i + W], sp_bufs, sp_tags, vw,
                            sp_idn)
                vw = []
            snap_res2 = cV.count
            xf_V_free[0] = snap_res2
            g_guard[0] = g_spk_done
            for j in range(4):
                if spk_bank_free[j]:
                    bank_free[4 + j] = spk_bank_free[j]

            for tb in range(TB if PH >= 7 else 0):
                par = tb % 2
                csl = slice(tb * 128, (tb + 1) * 128)
                ssnap = 0
                for c in range(DC):
                    bi_ = c % 4
                    bank = ps[bi_]
                    P.op("t", lambda e, bank=bank, c=c, csl=csl: e.transpose(
                        bank[:, 0:128], Ur[:, c, csl], c_ident[:]),
                        waits=[(cV, snap_res2), (cG, g_guard[0]),
                               (cDC, snap_consts)] + bank_free[bi_],
                        inc=cT)
                    if c % 2 == 0:
                        P.op("v", lambda e, bank=bank, c=c, par=par: e.tensor_copy(
                            tm[par][:, c * 128:(c + 1) * 128], bank[:, 0:128]),
                            waits=[(cT, cT.count)], reads=(f"ps{bi_}",),
                            writes=(f"tm{par}",), inc=cV)
                        bank_free[bi_] = [(cV, cV.count)]
                    else:
                        P.op("s", lambda e, bank=bank, c=c, par=par: e.copy(
                            tm[par][:, c * 128:(c + 1) * 128], bank[:, 0:128]),
                            waits=[(cT, cT.count)], reads=(f"ps{bi_}",),
                            writes=(f"tm{par}",), inc=cS)
                        ssnap = cS.count
                        bank_free[bi_] = [(cS, cS.count)]
                P.op("v", lambda e, par=par: e.tensor_reduce(
                    nsum[:], tm[par][:], AX.X, ALU.add, negate=True),
                    waits=[(cS, ssnap)] if ssnap else [],
                    reads=(f"tm{par}",), writes=("nsum",))
                P.op("v", lambda e: e.tensor_scalar_mul(negm[:], nsum[:], 1.0 / D),
                     reads=("nsum",), writes=("negm",), inc=cV)
                P.op("s", lambda e, par=par: e.activation(
                    tm[1 - par][:], tm[par][:], AF.Square, bias=negm[:],
                    accum_out=ssq[:]),
                    waits=[(cV, cV.count)], reads=(f"tm{par}", "negm"),
                    writes=(f"tm{1 - par}", "ssq"), inc=cS)
                P.op("v", lambda e: e.tensor_scalar(
                    vepsc[:], ssq[:], 1.0 / D, 1e-5, ALU.mult, ALU.add),
                    waits=[(cS, cS.count)], reads=("ssq",), writes=("vepsc",),
                    inc=cV)
                P.op("s", lambda e: e.activation(sqv[:], vepsc[:], AF.Sqrt),
                     waits=[(cV, cV.count)], reads=("vepsc",), writes=("sqv",),
                     inc=cS)
                P.op("v", lambda e: e.reciprocal(r2c[:], sqv[:]),
                     waits=[(cS, cS.count)], reads=("sqv",), writes=("r2c",))
                P.op("v", lambda e: e.tensor_scalar_mul(m2c[:], negm[:], -1.0),
                     reads=("negm",), writes=("m2c",), inc=cV)
                P.op("v", lambda e, par=par: e.tensor_scalar(
                    ob[par][:], tm[par][:], m2c[:], r2c[:], ALU.subtract, ALU.mult),
                    waits=[(cO[0], cO[0].count), (cO[1], cO[1].count),
                           (cG, g_guard[0])],
                    reads=(f"tm{par}", "m2c", "r2c"), writes=("ob0",), inc=cV)
                obsnap = cV.count
                P.op("g", lambda e, tb=tb, par=par, bcol=bcol: e.dma_start(
                    out=out[bcol + tb * 128: bcol + (tb + 1) * 128, :],
                    in_=ob[par][:]),
                    waits=[(cV, obsnap)], inc=cO[par], inc_amt=16)

        total_out = [cO[0].count, cO[1].count]
        with nc.Block() as block:
            @block.tensor
            def _(eng):
                P.emit("t", eng)

            @block.vector
            def _(eng):
                P.emit("v", eng)

            @block.scalar
            def _(eng):
                P.emit("s", eng)

            @block.sync
            def _(eng):
                P.emit("y", eng)

            @block.gpsimd
            def _(eng):
                P.emit("g", eng)
                if total_out[0]:
                    eng.wait_ge(cO[0].sem, total_out[0])
                if total_out[1]:
                    eng.wait_ge(cO[1].sem, total_out[1])

    return nc


def _mk_selh():
    s = np.zeros((33, 128), dtype=np.float32)
    s[0, 0:64] = 1.0
    s[32, 64:128] = 1.0
    return s


def kernel(L0_attn_x, L0_attn_c, L0_attn_Wq, L0_attn_bq, L0_attn_Wk, L0_attn_bk,
           L0_attn_Wv, L0_attn_bv, L0_attn_Wo, L0_attn_bo, L0_attn_rel_table,
           L0_ffn_W_fc, L0_ffn_W_proj):
    x = np.ascontiguousarray(np.asarray(L0_attn_x, dtype=np.float32))
    B, S, D = x.shape
    H = 16
    F = np.asarray(L0_ffn_W_fc).shape[0]
    NB = B // N_CORES
    cfg = make_cfg(S=S, D=D, H=H, F=F, NB=NB)

    rel = np.asarray(L0_attn_rel_table, np.float32)
    shared = dict(
        wq=pack_weight_strips_qkv(np.asarray(L0_attn_Wq, np.float32), D),
        wk=pack_weight_strips_qkv(np.asarray(L0_attn_Wk, np.float32), D),
        wv=pack_weight_strips_qkv(np.asarray(L0_attn_Wv, np.float32), D),
        wo=pack_weight_strips_qkv(np.asarray(L0_attn_Wo, np.float32), D),
        wfc=pack_wfc_strips(np.asarray(L0_ffn_W_fc, np.float32), D, F),
        wpr=pack_wpr_strips(np.asarray(L0_ffn_W_proj, np.float32), D, F),
        bqc=pack_bias_cols(np.asarray(L0_attn_bq, np.float32), D),
        bkc=pack_bias_cols(np.asarray(L0_attn_bk, np.float32), D),
        bvc=pack_bias_cols(np.asarray(L0_attn_bv, np.float32), D),
        boc=pack_bias_cols(np.asarray(L0_attn_bo, np.float32), D),
        crep=np.ascontiguousarray(
            np.repeat(rel[cfg["CLIP"]:cfg["CLIP"] + 1, :], 128, 0)),
        bands=pack_bands(rel, cfg),
        ident=np.eye(128, dtype=np.float32),
        selh=_mk_selh(),
    )
    in_maps = []
    for core in range(N_CORES):
        xc = x[core * NB:(core + 1) * NB]
        m = dict(shared)
        m["xT"] = np.ascontiguousarray(xc.transpose(2, 0, 1).reshape(D, NB * S))
        in_maps.append(m)

    nc = build_program(cfg)
    res = run_bass_kernel_spmd(nc, in_maps, core_ids=list(range(N_CORES)))
    outs = [np.asarray(r["out"]).reshape(NB, S, D) for r in res.results]
    return np.ascontiguousarray(np.concatenate(outs, axis=0))


# version 19
# speedup vs baseline: 1.1194x; 1.1194x over previous
import numpy as np
from contextlib import ExitStack

import concourse.bass as bass
import concourse.mybir as mybir
from concourse.bass_utils import run_bass_kernel_spmd

F32 = mybir.dt.float32
AF = mybir.ActivationFunctionType
ALU = mybir.AluOpType
AX = mybir.AxisListType

N_CORES = 8


def make_cfg(S=2048, D=1024, H=16, F=4096, NB=2):
    dk = D // H
    assert dk == 64 and D % 128 == 0 and S % 512 == 0 and F % 256 == 0 and H % 2 == 0
    return dict(
        S=S, D=D, H=H, F=F, NB=NB, dk=dk,
        DC=D // 128, QT=S // 512, KC=S // 128,
        FB=F // 256, TB=S // 128, HP=H // 2,
        REL=2 * 512 - 1, CLIP=2 * 512 - 2,
    )


def band_map(cfg):
    S, CLIP = cfg["S"], cfg["CLIP"]
    m = {}
    for qt in range(cfg["QT"]):
        for kc in range(cfg["KC"]):
            base = S - 1 + qt * 512 - kc * 128
            if base - 127 < CLIP:
                m[(kc, qt)] = base
    return m, sorted(set(m.values()))


def pack_weight_strips_qkv(W, D):
    DC = D // 128
    HPn = W.shape[1] // 128
    w = W.reshape(DC, 128, HPn, 128)
    return np.ascontiguousarray(w.transpose(2, 1, 0, 3))


def pack_wfc_strips(Wfc, D, F):
    DC = D // 128
    w = Wfc.reshape(F // 256, 2, 128, DC, 128)
    return np.ascontiguousarray(w.transpose(0, 1, 4, 3, 2))


def pack_wpr_strips(Wpr, D, F):
    DC = D // 128
    w = Wpr.reshape(DC, 128, F // 256, 2, 128)
    return np.ascontiguousarray(w.transpose(2, 0, 4, 3, 1))


def pack_bias_cols(b, D):
    return np.ascontiguousarray(b.reshape(D // 128, 128).T)


def pack_bands(rel, cfg):
    _, bases = band_map(cfg)
    CLIP = cfg["CLIP"]
    jj = np.arange(128)[:, None]
    ii = np.arange(512)[None, :]
    out = np.empty((cfg["H"], max(len(bases), 1), 128, 512), dtype=np.float32)
    for bi, base in enumerate(bases):
        idx = np.minimum(base + ii - jj, CLIP)
        out[:, bi] = rel[idx, :].transpose(2, 0, 1)
    return np.ascontiguousarray(out)


class Chan:
    __slots__ = ("name", "sem", "count")

    def __init__(self, name, sem):
        self.name, self.sem, self.count = name, sem, 0


class Prog:
    ENGINES = ("t", "v", "s", "g", "y")

    def __init__(self):
        self.ops = []
        self.wr = {e: set() for e in self.ENGINES}
        self.rd = {e: set() for e in self.ENGINES}
        self.n_drain = 0

    def op(self, eng, fn, waits=(), inc=None, inc_amt=1, reads=(), writes=()):
        for ch, val in waits:
            assert val <= ch.count, (ch.name, val, ch.count)
        if eng in ("v", "s", "g"):
            r, w = set(reads), set(writes)
            if (r | w) & self.wr[eng] or (w & self.rd[eng]):
                self.ops.append((eng, "drain", None))
                self.wr[eng].clear()
                self.rd[eng].clear()
                self.n_drain += 1
            self.rd[eng] |= r
            self.wr[eng] |= w
        self.ops.append((eng, "op", (fn, tuple(waits), inc, inc_amt)))
        if inc is not None:
            inc.count += inc_amt

    def emit(self, eng_key, eng):
        done = {}
        for e, kind, payload in self.ops:
            if e != eng_key:
                continue
            if kind == "drain":
                eng.drain()
                continue
            fn, waits, inc, inc_amt = payload
            for ch, val in waits:
                if val > 0 and done.get(id(ch), -1) < val:
                    eng.wait_ge(ch.sem, val)
                    done[id(ch)] = val
            instr = fn(eng)
            if inc is not None:
                instr.then_inc(inc.sem, inc_amt)


def build_program(cfg):
    S, D, H, F, NB = cfg["S"], cfg["D"], cfg["H"], cfg["F"], cfg["NB"]
    DC, QT, KC, FB, TB, HP = (cfg[k] for k in ("DC", "QT", "KC", "FB", "TB", "HP"))
    PH = cfg.get("ph", 99)
    bmap, bases = band_map(cfg)
    base_idx = {v: i for i, v in enumerate(bases)}
    NBASE = max(len(bases), 1)

    nc = bass.Bass("TRN2", target_bir_lowering=False, debug=False)

    xT = nc.dram_tensor("xT", [D, NB * S], F32, kind="ExternalInput").ap()
    wq = nc.dram_tensor("wq", [HP, 128, DC, 128], F32, kind="ExternalInput").ap()
    wk = nc.dram_tensor("wk", [HP, 128, DC, 128], F32, kind="ExternalInput").ap()
    wv = nc.dram_tensor("wv", [HP, 128, DC, 128], F32, kind="ExternalInput").ap()
    wo = nc.dram_tensor("wo", [DC, 128, DC, 128], F32, kind="ExternalInput").ap()
    wfc = nc.dram_tensor("wfc", [FB, 2, 128, DC, 128], F32, kind="ExternalInput").ap()
    wpr = nc.dram_tensor("wpr", [FB, DC, 128, 2, 128], F32, kind="ExternalInput").ap()
    bqc = nc.dram_tensor("bqc", [128, DC], F32, kind="ExternalInput").ap()
    bkc = nc.dram_tensor("bkc", [128, DC], F32, kind="ExternalInput").ap()
    bvc = nc.dram_tensor("bvc", [128, DC], F32, kind="ExternalInput").ap()
    boc = nc.dram_tensor("boc", [128, DC], F32, kind="ExternalInput").ap()
    crep = nc.dram_tensor("crep", [128, H], F32, kind="ExternalInput").ap()
    bands = nc.dram_tensor("bands", [H, NBASE, 128, 512], F32, kind="ExternalInput").ap()
    ident = nc.dram_tensor("ident", [128, 128], F32, kind="ExternalInput").ap()
    selh = nc.dram_tensor("selh", [33, 128], F32, kind="ExternalInput").ap()
    out = nc.dram_tensor("out", [NB * S, D], F32, kind="ExternalOutput").ap()

    with ExitStack() as ctx:
        def sb(name, shape, dt=F32):
            return ctx.enter_context(nc.sbuf_tensor(name, shape, dt))

        c_ident = sb("c_ident", [128, 128])
        c_identb = sb("c_identb", [128, 128], mybir.dt.bfloat16)
        selbc = sb("selbc", [33, 128])
        c_crep = sb("c_crep", [128, H])
        c_bq = sb("c_bq", [128, DC]); c_bk = sb("c_bk", [128, DC])
        c_bv = sb("c_bv", [128, DC]); c_bo = sb("c_bo", [128, DC])
        onescol = sb("onescol", [128, 1])
        onesrow = sb("onesrow", [1, 128])
        BF16 = mybir.dt.bfloat16
        xhi = sb("xhi", [128, DC, S], BF16)
        xlo = sb("xlo", [128, DC, S], BF16)
        xpc = [sb(f"xpc{i}", [128, 512]) for i in range(2)]
        wbf = [sb(f"wbf{i}", [128, DC * 128]) for i in range(2)]
        whi = [sb(f"whi{i}", [128, DC * 128], BF16) for i in range(2)]
        wlo = [sb(f"wlo{i}", [128, DC * 128], BF16) for i in range(2)]
        bbuf = [sb(f"bbuf{i}", [128, 512]) for i in range(2)]
        bscr = [sb(f"bscr{i}", [128, 512]) for i in range(2)]
        qhi = sb("qhi", [128, S], BF16); qlo = sb("qlo", [128, S], BF16)
        khi = sb("khi", [128, S], BF16); klo = sb("klo", [128, S], BF16)
        vtiles = sb("vtiles", [128, KC, 130])
        eA = [sb(f"eA{i}", [128, 512]) for i in range(2)]
        eB = [sb(f"eB{i}", [128, 512]) for i in range(2)]
        Ur = sb("Ur", [128, DC, S])
        rscr = [sb(f"rscr{i}", [128, 512]) for i in range(2)]
        tm = [sb(f"tm{i}", [128, D]) for i in range(2)]
        ob = [sb("ob0", [128, D])]
        ob = [ob[0], ob[0]]
        nsum = sb("nsum", [128, 1]); negm = sb("negm", [128, 1])
        m2c = sb("m2c", [128, 1]); r2c = sb("r2c", [128, 1])
        sqv = sb("sqv", [128, 1]); ssq = sb("ssq", [128, 1])
        vepsc = sb("vepsc", [128, 1])

        ps = [ctx.enter_context(nc.psum_tensor(f"ps{i}", [128, 512], F32))
              for i in range(8)]

        def chan(name):
            return Chan(name, ctx.enter_context(nc.semaphore(name)))

        cT = chan("cT"); cV = chan("cV"); cS = chan("cS"); cG = chan("cG")
        cDC = chan("cDC")
        cW = [chan("cW0"), chan("cW1")]
        cB = [chan("cB0"), chan("cB1")]
        cX = chan("cX"); cX2 = chan("cX2")
        cO = [chan("cO0"), chan("cO1")]

        P = Prog()

        def mm(e, o, l, r, st, sp, tp=None):
            return e.matmul(o, l, r, start=st, stop=sp, tile_position=tp)

        bank_free = {i: [] for i in range(8)}

        for dst, src in ((c_ident, ident), (c_crep, crep), (c_bq, bqc),
                         (c_bk, bkc), (c_bv, bvc), (c_bo, boc),
                         (selbc, selh)):
            P.op("y", lambda e, d=dst, s=src: e.dma_start(out=d[:], in_=s[:]),
                 inc=cDC, inc_amt=16)
        snap_consts = cDC.count
        P.op("s", lambda e: e.copy(c_identb[:], c_ident[:]),
             waits=[(cDC, snap_consts)], writes=("c_identb",), inc=cS)
        snap_identb = cS.count
        P.op("v", lambda e: e.memset(onesrow[:], 1.0), writes=("onesrow",))
        P.op("v", lambda e: e.memset(onescol[:], 1.0), writes=("onescol",))
        P.op("v", lambda e: e.memset(
            vtiles.ap().rearrange("p k (h f) -> p k h f", f=65)[:, :, :, 64:65], 1.0),
            inc=cV, writes=("vtiles",))

        strip_user = {0: [], 1: []}
        strip_split = {0: [], 1: []}
        sidx = [0]

        def load_strip(src_ap, ncols, split=False):
            par = sidx[0] % 2
            sidx[0] += 1
            P.op("y", lambda e: e.dma_start(out=wbf[par][:, 0:ncols], in_=src_ap),
                 waits=list(strip_user[par]) + list(strip_split[par]),
                 inc=cW[par], inc_amt=16)
            dwsnap = cW[par].count
            if split:
                P.op("s", lambda e: e.copy(whi[par][:, 0:ncols],
                                           wbf[par][:, 0:ncols]),
                     waits=[(cW[par], dwsnap)] + list(strip_user[par]),
                     reads=(f"wbf{par}",), writes=(f"whi{par}",), inc=cS)
                hisnap = cS.count
                P.op("v", lambda e: e.tensor_tensor(
                    wlo[par][:, 0:ncols], wbf[par][:, 0:ncols],
                    whi[par][:, 0:ncols], ALU.subtract),
                    waits=[(cS, hisnap)],
                    reads=(f"wbf{par}", f"whi{par}"), writes=(f"wlo{par}",),
                    inc=cV)
                strip_split[par] = [(cV, cV.count)]
                return par, [(cS, hisnap), (cV, cV.count)]
            strip_split[par] = []
            return par, [(cW[par], dwsnap)]

        xf_T_free = [0]
        xf_V_free = [0]
        g_guard = [0]
        xsplit = [0, 0]
        for b in range(NB):
            bcol = b * S

            xw0 = ([(cT, xf_T_free[0]), (cV, xf_V_free[0])] if b else [])
            cXp = [cX, cX2]
            for i_ in range(DC * QT):
                c, t4 = divmod(i_, QT)
                pi = i_ % 2
                tsl4 = slice(t4 * 512, (t4 + 1) * 512)
                P.op("y", lambda e, c=c, t4=t4, bcol=bcol, pi=pi: e.dma_start(
                    out=xpc[pi][:],
                    in_=xT[c * 128:(c + 1) * 128,
                          bcol + t4 * 512: bcol + (t4 + 1) * 512]),
                    waits=xw0 + ([(cV, xsplit[pi])] if xsplit[pi] else []),
                    inc=cXp[pi], inc_amt=16)
                xw0 = []
                P.op("s", lambda e, c=c, tsl4=tsl4, pi=pi: e.copy(
                    xhi[:, c, tsl4], xpc[pi][:]),
                    waits=[(cXp[pi], cXp[pi].count)] +
                          ([(cT, xf_T_free[0]), (cV, xf_V_free[0])]
                           if (b and i_ == 0) else []),
                    reads=(f"xpc{pi}",), writes=(f"xhi{c}",), inc=cS)
                P.op("v", lambda e, c=c, tsl4=tsl4, pi=pi: e.tensor_tensor(
                    xlo[:, c, tsl4], xpc[pi][:], xhi[:, c, tsl4], ALU.subtract),
                    waits=[(cS, cS.count)],
                    reads=(f"xpc{pi}", f"xhi{c}"), writes=(f"xlo{c}",), inc=cV)
                xsplit[pi] = cV.count
            snap_x = cV.count

            band_guard = [0]
            bscr_guard = [0, 0]
            pending_fin = [None]

            def emit_finish():
                if pending_fin[0] is not None:
                    f = pending_fin[0]
                    pending_fin[0] = None
                    f()
            nHP = HP if PH >= 1 else 0
            if nHP:
                nxt_qk = (load_strip(wq[0].rearrange("p c f -> p (c f)"),
                                     DC * 128, split=True),
                          load_strip(wk[0].rearrange("p c f -> p (c f)"),
                                     DC * 128, split=True))
            for hp in range(nHP):
                hA, hB = 2 * hp, 2 * hp + 1
                pq, pk = nxt_qk

                for pj, ((par, dwsnap), dhi, dlo, bias_sb, dtag) in enumerate(
                        ((pq, qhi, qlo, c_bq, "q"), (pk, khi, klo, c_bk, "k"))):
                    if pj == 1:
                        pv_ = load_strip(wv[hp].rearrange("p c f -> p (c f)"),
                                         DC * 128, split=True)
                    for t in range(QT):
                        bi_ = t % 2
                        bank = ps[bi_]
                        wts = [(cV, snap_x)] + list(dwsnap) + bank_free[bi_]
                        first = True
                        for c in range(DC):
                            for p_ in range(3):
                                lh = (whi, whi, wlo)[p_][par]
                                rh = (xhi, xlo, xhi)[p_]
                                last = (p_ == 2 and c == DC - 1)
                                P.op("t", lambda e, lh=lh, rh=rh, c=c, t=t,
                                     bank=bank, f=first, last=last: mm(
                                         e, bank[:], lh[:, c * 128:(c + 1) * 128],
                                         rh[:, c, t * 512:(t + 1) * 512], f, last),
                                     waits=wts if first else (),
                                     inc=cT if last else None)
                                first = False
                                wts = ()
                        mmsnap = cT.count
                        hs = dhi[:, t * 512:(t + 1) * 512]
                        ls = dlo[:, t * 512:(t + 1) * 512]
                        P.op("v", lambda e, bank=bank, hs=hs, bb=bias_sb, hp=hp:
                             e.tensor_scalar_add(hs, bank[:], bb[:, hp:hp + 1]),
                             waits=[(cT, mmsnap), (cDC, snap_consts)],
                             reads=(f"ps{bi_}",), writes=(f"{dtag}hi{t}",))
                        P.op("v", lambda e, bank=bank, hs=hs, ls=ls, bb=bias_sb,
                             hp=hp: e.scalar_tensor_tensor(
                                 ls, bank[:], bb[:, hp:hp + 1], hs,
                                 ALU.add, ALU.subtract),
                             reads=(f"ps{bi_}", f"{dtag}hi{t}"),
                             writes=(f"{dtag}lo{t}",), inc=cV)
                        bank_free[bi_] = [(cV, cV.count)]
                        if pj == 0 and t == 0:
                            emit_finish()
                    strip_user[par] = [(cT, mmsnap)]

                par, dwsnap = pv_
                for t in range(QT):
                    bi_ = 4 + t % 2
                    bank = ps[bi_]
                    wts = [(cV, snap_x)] + list(dwsnap) + bank_free[bi_]
                    first = True
                    for c in range(DC):
                        for p_ in range(3):
                            lh = (whi, whi, wlo)[p_][par]
                            rh = (xhi, xlo, xhi)[p_]
                            last = (p_ == 2 and c == DC - 1)
                            P.op("t", lambda e, lh=lh, rh=rh, c=c, t=t, bank=bank,
                                 f=first, last=last: mm(
                                     e, bank[:], lh[:, c * 128:(c + 1) * 128],
                                     rh[:, c, t * 512:(t + 1) * 512], f, last),
                                 waits=wts if first else (),
                                 inc=cT if last else None)
                            first = False
                            wts = ()
                    mmsnap = cT.count
                    P.op("v", lambda e, bank=bank, hp=hp: e.tensor_scalar_add(
                        rscr[0][:], bank[:], c_bv[:, hp:hp + 1]),
                        waits=[(cT, mmsnap), (cDC, snap_consts)],
                        reads=(f"ps{bi_}",), writes=("rscr0",), inc=cV)
                    bank_free[bi_] = [(cV, cV.count)]
                    vsn = cV.count
                    for q4 in range(4):
                        kc = t * 4 + q4
                        ti_ = 6 + q4 % 2
                        tb_ = ps[ti_]
                        P.op("t", lambda e, tb_=tb_, q4=q4: e.transpose(
                            tb_[:, 0:128], rscr[0][:, q4 * 128:(q4 + 1) * 128],
                            c_ident[:]),
                            waits=[(cV, vsn), (cDC, snap_consts)] + bank_free[ti_],
                            inc=cT)
                        dst = vtiles.ap().rearrange(
                            "p k (h f) -> p k h f", f=65)[:, kc, :, 0:64]
                        P.op("v", lambda e, tb_=tb_, dst=dst: e.tensor_copy(
                            dst, tb_.ap().rearrange(
                                "p (h f) -> p h f", f=64)[:, 0:2, :]),
                            waits=[(cT, cT.count)], reads=(f"ps{ti_}",),
                            writes=("vtiles",), inc=cV)
                        bank_free[ti_] = [(cV, cV.count)]
                strip_user[par] = [(cT, mmsnap)]
                qkv_ready = cV.count
                if hp + 1 < nHP:
                    nxt_qk = (load_strip(wq[hp + 1].rearrange("p c f -> p (c f)"),
                                         DC * 128, split=True),
                              load_strip(wk[hp + 1].rearrange("p c f -> p (c f)"),
                                         DC * 128, split=True))

                for qt in range(QT if PH >= 2 else 0):
                    qsl = slice(qt * 512, (qt + 1) * 512)
                    s_snaps, exp_snaps, pv_snaps = {}, {}, {}

                    def issue_scores(kc, head=None):
                        par_ = kc % 2
                        heads = ((0, 0), (1, 64)) if head is None else \
                                (((0, 0),) if head == 0 else ((1, 64),))
                        wts0 = []
                        for h_i, r0 in heads:
                            wts0 += bank_free[2 * h_i + par_]
                        if kc < 2:
                            wts0 += [(cV, qkv_ready)]
                        for h_i, r0 in heads:
                            for p_ in range(3):
                                lhK = (khi, khi, klo)[p_]
                                rhQ = (qhi, qlo, qhi)[p_]
                                bi_ = 2 * h_i + par_
                                bank = ps[bi_]
                                P.op("t", lambda e, lhK=lhK, rhQ=rhQ, bank=bank,
                                     r0=r0, kc=kc, qsl=qsl, p_=p_: mm(
                                         e, bank[:],
                                         lhK[r0:r0 + 64, kc * 128:(kc + 1) * 128],
                                         rhQ[r0:r0 + 64, qsl], p_ == 0, p_ == 2,
                                         tp=(r0, 0)),
                                     waits=wts0,
                                     inc=cT if (p_ == 2 and
                                                h_i == heads[-1][0]) else None)
                                wts0 = ()
                        for h_i, _ in heads:
                            s_snaps[(kc, h_i)] = cT.count

                    def prefetch_band(kc):
                        if (kc, qt) not in bmap:
                            return
                        bi = base_idx[bmap[(kc, qt)]]
                        wdma = [(cV, band_guard[0])] if band_guard[0] else []
                        if g_guard[0]:
                            wdma.append((cG, g_guard[0]))
                        for h_i, h in enumerate((hA, hB)):
                            P.op("y", lambda e, h=h, bi=bi, h_i=h_i: e.dma_start(
                                out=bbuf[h_i][:], in_=bands[h, bi]),
                                waits=wdma, inc=cB[h_i], inc_amt=16)
                            wdma = []

                    def issue_exp(kc):
                        par_ = kc % 2
                        banded = (kc, qt) in bmap
                        guard = [(cT, pv_snaps[(kc - 2, 1)])] if kc >= 2 else []
                        for h_i, (h, ebuf) in enumerate(
                                ((hA, eA[par_]), (hB, eB[par_]))):
                            bi_ = 2 * h_i + par_
                            bank = ps[bi_]
                            etag = f"e{'AB'[h_i]}{par_}"
                            if banded:
                                P.op("v", lambda e, bank=bank, h_i=h_i:
                                     e.scalar_tensor_tensor(
                                         bscr[h_i][:], bank[:], 0.125, bbuf[h_i][:],
                                         ALU.mult, ALU.add),
                                     waits=[(cT, s_snaps[(kc, h_i)]),
                                            (cB[h_i], cB[h_i].count),
                                            (cS, bscr_guard[h_i])],
                                     reads=(f"ps{bi_}", f"bbuf{h_i}"),
                                     writes=(f"bscr{h_i}",), inc=cV)
                                band_guard[0] = cV.count
                                bank_free[bi_] = [(cV, cV.count)]
                                P.op("s", lambda e, ebuf=ebuf, h_i=h_i: e.activation(
                                    ebuf[:], bscr[h_i][:], AF.Exp),
                                    waits=[(cV, cV.count)] + guard,
                                    reads=(f"bscr{h_i}",), writes=(etag,), inc=cS)
                                bscr_guard[h_i] = cS.count
                            else:
                                P.op("s", lambda e, bank=bank, ebuf=ebuf, h=h:
                                     e.activation(ebuf[:], bank[:], AF.Exp,
                                                  bias=c_crep[:, h:h + 1],
                                                  scale=0.125),
                                     waits=[(cT, s_snaps[(kc, h_i)]),
                                            (cDC, snap_consts)] + guard,
                                     reads=(f"ps{bi_}",), writes=(etag,), inc=cS)
                                bank_free[bi_] = [(cS, cS.count)]
                            exp_snaps[(kc, h_i)] = cS.count

                    ub0 = 4 if qt % 2 == 0 else 6

                    def issue_pv(kc, h_i):
                        par_ = kc % 2
                        ub, ebuf = ((ps[ub0], eA[par_]),
                                    (ps[ub0 + 1], eB[par_]))[h_i]
                        hsl = slice(h_i * 65, h_i * 65 + 65)
                        wts = [(cS, exp_snaps[(kc, h_i)])]
                        if kc == 0:
                            wts += bank_free[ub0 + h_i]
                        P.op("t", lambda e, ub=ub, ebuf=ebuf, kc=kc, hsl=hsl: mm(
                            e, ub[0:65, :], vtiles[:, kc, hsl], ebuf[:],
                            kc == 0, kc == KC - 1),
                            waits=wts, inc=cT)
                        pv_snaps[(kc, h_i)] = cT.count

                    issue_scores(0)
                    prefetch_band(0)
                    for kc in range(KC):
                        if kc + 1 < KC:
                            issue_scores(kc + 1)
                        if kc == min(4, KC - 1):
                            emit_finish()
                        issue_exp(kc)
                        if kc + 1 < KC:
                            prefetch_band(kc + 1)
                        issue_pv(kc, 0)
                        issue_pv(kc, 1)

                    def make_finish(hp=hp, qsl=qsl, ub0=ub0, pv_last=None):
                        pv_last = pv_snaps[(KC - 1, 1)]

                        def fin():
                            for h_i in range(2):
                                P.op("v", lambda e, ub=ps[ub0 + h_i], h_i=h_i:
                                     e.reciprocal(
                                         xpc[0][32 * h_i:32 * h_i + 1, :],
                                         ub[64:65, :]),
                                     waits=[(cT, pv_last)],
                                     reads=(f"ps{ub0 + h_i}",),
                                     writes=("xpc0",), inc=cV)
                            rsnap = cV.count
                            for h_i in range(2):
                                P.op("t", lambda e, ub=ps[ub0 + h_i], h_i=h_i:
                                     mm(e, ub[64:128, :],
                                        selbc[:, h_i * 64:(h_i + 1) * 64],
                                        xpc[0][0:33, :], True, True),
                                     waits=[(cV, rsnap), (cDC, snap_consts)],
                                     inc=cT)
                            bcT = cT.count
                            for h_i in range(2):
                                bcs = rscr[h_i][0:64, :]
                                P.op("v", lambda e, ub=ps[ub0 + h_i], bcs=bcs:
                                     e.tensor_copy(bcs, ub[64:128, :]),
                                     waits=[(cT, bcT)],
                                     reads=(f"ps{ub0 + h_i}",),
                                     writes=(f"rscr{h_i}",))
                                dst = Ur[:, hp, qsl][h_i * 64:(h_i + 1) * 64, :]
                                P.op("v", lambda e, ub=ps[ub0 + h_i], bcs=bcs,
                                     dst=dst: e.tensor_tensor(
                                         dst, ub[0:64, :], bcs, ALU.mult),
                                     reads=(f"ps{ub0 + h_i}", f"rscr{h_i}"),
                                     writes=(f"Ur{hp}",), inc=cV)
                                bank_free[ub0 + h_i] = [(cV, cV.count)]
                        return fin

                    pending_fin[0] = make_finish()
            emit_finish()
            snap_attn = cV.count

            res1_t = {}
            for t in range(QT if PH >= 3 else 0):
                tsl = slice(t * 512, (t + 1) * 512)
                lastgrp = 0
                for o in range(DC):
                    par, dwsnap = load_strip(
                        wo[o].rearrange("p c f -> p (c f)"), DC * 128)
                    wts = list(dwsnap) + [(cV, snap_attn)] + bank_free[o]
                    for c in range(DC):
                        P.op("t", lambda e, c=c, o=o, par=par, tsl=tsl: mm(
                            e, ps[o][:], wbf[par][:, c * 128:(c + 1) * 128],
                            Ur[:, c, tsl], c == 0, c == DC - 1),
                            waits=wts if c == 0 else (),
                            inc=cT if c == DC - 1 else None)
                        wts = ()
                    strip_user[par] = [(cT, cT.count)]
                    lastgrp = cT.count
                for o in range(DC):
                    P.op("v", lambda e, o=o, tsl=tsl: e.scalar_tensor_tensor(
                        rscr[o % 2][:], ps[o][:], c_bo[:, o:o + 1], xhi[:, o, tsl],
                        ALU.add, ALU.add),
                        waits=[(cT, lastgrp), (cDC, snap_consts)],
                        reads=(f"ps{o}", f"xhi{o}"), writes=(f"rscr{o % 2}",))
                    P.op("v", lambda e, o=o, tsl=tsl: e.tensor_tensor(
                        Ur[:, o, tsl], rscr[o % 2][:], xlo[:, o, tsl], ALU.add),
                        reads=(f"rscr{o % 2}", f"xlo{o}"), writes=(f"Ur{o}",),
                        inc=cV)
                    bank_free[o] = [(cV, cV.count)]
                res1_t[t] = cV.count
            snap_res1 = cV.count

            nFB = FB if PH >= 5 else 0
            strip_plan = []
            for fb_ in range(nFB):
                strip_plan += [("fc", fb_, o_) for o_ in range(2)]
                strip_plan += [("pr", fb_, o_) for o_ in range(DC)]
            strip_idx = [0]

            def next_strip():
                kind, fb_, o_ = strip_plan[strip_idx[0]]
                strip_idx[0] += 1
                if kind == "fc":
                    return load_strip(wfc[fb_, o_].rearrange("p c f -> p (c f)"),
                                      DC * 128, split=True)
                return load_strip(wpr[fb_, o_].rearrange("p c f -> p (c f)"),
                                  256, split=True)

            pend = [next_strip() if strip_plan else None,
                    next_strip() if len(strip_plan) > 1 else None]

            def pop_strip():
                return pend.pop(0)

            def prefetch_strip():
                pend.append(next_strip()
                            if strip_idx[0] < len(strip_plan) else None)

            snap_x1_t = {}
            QTn = QT if PH >= 4 else 0
            sqk = [0]
            sqmm = {}
            for t in range(QTn):
                tsl = slice(t * 512, (t + 1) * 512)
                wts = [(cV, res1_t.get(t, snap_res1))] + bank_free[t]
                for c in range(DC):
                    P.op("t", lambda e, c=c, tsl=tsl, t=t: mm(
                        e, ps[t][0:1, :], onescol[:], Ur[:, c, tsl],
                        c == 0, c == DC - 1),
                        waits=wts if c == 0 else (),
                        inc=cT if c == DC - 1 else None)
                    wts = ()
                bw = list(bank_free[4 + t])
                for c in range(DC):
                    k = sqk[0]
                    sqk[0] += 1
                    w = [(cV, res1_t.get(t, snap_res1)), (cV, cV.count)]
                    if k >= 2:
                        w.append((cT, sqmm[k - 2]))
                    P.op("s", lambda e, c=c, tsl=tsl, k=k: e.activation(
                        rscr[k % 2][:], Ur[:, c, tsl], AF.Square),
                        waits=w, reads=(f"Ur{c}",), writes=(f"rscr{k % 2}",),
                        inc=cS)
                    P.op("t", lambda e, c=c, t=t, k=k: mm(
                        e, ps[4 + t][0:1, :], onescol[:], rscr[k % 2][:],
                        c == 0, c == DC - 1),
                        waits=[(cS, cS.count)] + bw, inc=cT)
                    bw = []
                    sqmm[k] = cT.count
            if D >= 1024:
                rrows = [tm[0].ap()[0:1, 0:512], tm[0].ap()[0:1, 512:1024],
                         tm[1].ap()[0:1, 0:512], tm[1].ap()[0:1, 512:1024]]
                mrrows = [ob[0].ap()[0:1, 0:512], ob[0].ap()[0:1, 512:1024],
                          xpc[0][0:1, 0:512], xpc[1][0:1, 0:512]]
                rtags = ["tm0a", "tm0b", "tm1a", "tm1b"]
                mtags = ["ob0a", "ob0b", "xpc0", "xpc1"]
            else:
                rrows = [bbuf[0][0:1, 0:512], bbuf[1][0:1, 0:512]] * 2
                mrrows = [xpc[0][0:1, 0:512], xpc[1][0:1, 0:512]] * 2
                rtags = ["bbuf0", "bbuf1"] * 2
                mtags = ["xpc0", "xpc1"] * 2
            rowsnap_t = {}
            for t in range(QTn):
                mrow = bscr[0][0:1, 0:512]; vrow = bscr[1][0:1, 0:512]
                msqr = eA[1][0:1, 0:512]; veps = eA[0][0:1, 0:512]
                sqvr = eB[0][0:1, 0:512]
                rrow, mrrow = rrows[t % 4], mrrows[t % 4]
                rtg, mtg = rtags[t % 4], mtags[t % 4]
                P.op("v", lambda e, mrow=mrow, t=t: e.tensor_scalar_mul(
                    mrow, ps[t][0:1, :], 1.0 / D),
                    waits=[(cT, sqmm[(t + 1) * DC - 1])] +
                          ([(cO[0], cO[0].count), (cO[1], cO[1].count)]
                           if t == 0 else []),
                    reads=(f"ps{t}",), writes=("bscr0",), inc=cV)
                bank_free[t] = [(cV, cV.count)]
                P.op("s", lambda e, mrow=mrow, msqr=msqr: e.activation(
                    msqr, mrow, AF.Square),
                    waits=[(cV, cV.count)], reads=("bscr0",), writes=("eA1",),
                    inc=cS)
                P.op("v", lambda e, vrow=vrow, msqr=msqr, t=t:
                     e.scalar_tensor_tensor(
                         vrow, ps[4 + t][0:1, :], 1.0 / D, msqr,
                         ALU.mult, ALU.subtract),
                     waits=[(cS, cS.count)], reads=(f"ps{4 + t}", "eA1"),
                     writes=("bscr1",), inc=cV)
                bank_free[4 + t] = [(cV, cV.count)]
                P.op("v", lambda e, vrow=vrow, veps=veps: e.tensor_scalar(
                    veps, vrow, 1e-5, None, ALU.add),
                    reads=("bscr1",), writes=("eA0",), inc=cV)
                P.op("s", lambda e, veps=veps, sqvr=sqvr: e.activation(
                    sqvr, veps, AF.Sqrt),
                    waits=[(cV, cV.count)], reads=("eA0",), writes=("eB0",),
                    inc=cS)
                P.op("v", lambda e, rrow=rrow, sqvr=sqvr: e.reciprocal(rrow, sqvr),
                     waits=[(cS, cS.count)], reads=("eB0",), writes=(rtg,))
                P.op("v", lambda e, mrow=mrow, rrow=rrow, mrrow=mrrow:
                     e.tensor_tensor(mrrow, mrow, rrow, ALU.mult),
                     reads=("bscr0", rtg), writes=(mtg,), inc=cV)
                rowsnap_t[t] = cV.count
            bc_t = {}
            for t in range(QTn):
                P.op("t", lambda e, t=t: mm(
                    e, ps[t][:], onesrow[:], rrows[t % 4], True, True),
                    waits=[(cV, rowsnap_t[t])] + bank_free[t], inc=cT)
                P.op("t", lambda e, t=t: mm(
                    e, ps[4 + t][:], onesrow[:], mrrows[t % 4], True, True),
                    waits=bank_free[4 + t], inc=cT)
                bc_t[t] = cT.count
            for t in range(QTn):
                tsl = slice(t * 512, (t + 1) * 512)
                for c in range(DC):
                    P.op("v", lambda e, c=c, tsl=tsl, t=t: e.tensor_tensor(
                        rscr[c % 2][:], Ur[:, c, tsl], ps[t][:], ALU.mult),
                        waits=[(cT, bc_t[t])], reads=(f"Ur{c}", f"ps{t}"),
                        writes=(f"rscr{c % 2}",))
                    P.op("v", lambda e, c=c, tsl=tsl, t=t: e.tensor_tensor(
                        bscr[c % 2][:], rscr[c % 2][:], ps[4 + t][:],
                        ALU.subtract),
                        reads=(f"rscr{c % 2}", f"ps{4 + t}"),
                        writes=(f"bscr{c % 2}",))
                    P.op("v", lambda e, c=c, tsl=tsl: e.tensor_copy(
                        xhi[:, c, tsl], bscr[c % 2][:]),
                        reads=(f"bscr{c % 2}",), writes=(f"xhi{c}",))
                    P.op("v", lambda e, c=c, tsl=tsl: e.tensor_tensor(
                        xlo[:, c, tsl], bscr[c % 2][:], xhi[:, c, tsl],
                        ALU.subtract),
                        reads=(f"bscr{c % 2}", f"xhi{c}"), writes=(f"xlo{c}",),
                        inc=cV)
                bank_free[t] = [(cV, cV.count)]
                bank_free[4 + t] = [(cV, cV.count)]
                snap_x1_t[t] = cV.count
            snap_x1 = cV.count

            h_mm_snap = [0]
            rscr_free = {0: [(cV, snap_x1)], 1: [(cV, snap_x1)]}
            for fb in range(nFB):
                for o in range(2):
                    par, dwsnap = pop_strip()
                    hh = (qhi, qlo)[o]
                    hl = (khi, klo)[o]
                    htg = ("qhi", "qlo")[o]
                    ltg = ("khi", "klo")[o]
                    for t in range(QT):
                        tsl = slice(t * 512, (t + 1) * 512)
                        bi_ = (o * QT + t) % 4
                        bank = ps[bi_]
                        wts = list(dwsnap) + \
                              [(cV, snap_x1_t.get(t, snap_x1))] + bank_free[bi_]
                        if h_mm_snap[0]:
                            wts.append((cT, h_mm_snap[0]))
                        first = True
                        for c in range(DC):
                            for p_ in range(3):
                                lh = (whi, whi, wlo)[p_][par]
                                rh = (xhi, xlo, xhi)[p_]
                                last = (p_ == 2 and c == DC - 1)
                                P.op("t", lambda e, lh=lh, rh=rh, c=c, tsl=tsl,
                                     bank=bank, f=first, last=last: mm(
                                         e, bank[:], lh[:, c * 128:(c + 1) * 128],
                                         rh[:, c, tsl], f, last),
                                     waits=wts if first else (),
                                     inc=cT if last else None)
                                first = False
                                wts = ()
                        strip_user[par] = [(cT, cT.count)]
                        mmsnap = cT.count
                        P.op("s", lambda e, bank=bank, o=o: e.activation(
                            rscr[o][:], bank[:], AF.Relu),
                            waits=[(cT, mmsnap)] + rscr_free[o],
                            reads=(f"ps{bi_}",), writes=(f"rscr{o}",))
                        P.op("s", lambda e, hh=hh, tsl=tsl, o=o: e.activation(
                            hh[:, tsl], rscr[o][:], AF.Square),
                            reads=(f"rscr{o}",), writes=(f"{htg}h{t}",), inc=cS)
                        P.op("v", lambda e, bank=bank, o=o: e.tensor_tensor(
                            bscr[o][:], rscr[o][:], bank[:], ALU.mult),
                            waits=[(cS, cS.count)], reads=(f"rscr{o}", f"ps{bi_}"),
                            writes=(f"bscr{o}",))
                        P.op("v", lambda e, hh=hh, hl=hl, tsl=tsl, o=o:
                             e.tensor_tensor(hl[:, tsl], bscr[o][:], hh[:, tsl],
                                             ALU.subtract),
                             reads=(f"bscr{o}", f"{htg}h{t}"),
                             writes=(f"{ltg}h{t}",), inc=cV)
                        rscr_free[o] = [(cV, cV.count)]
                        bank_free[bi_] = [(cV, cV.count)]
                    prefetch_strip()
                fc_done = cV.count
                for o in range(DC):
                    par, dwsnap = pop_strip()
                    for t in range(QT):
                        tsl = slice(t * 512, (t + 1) * 512)
                        bi_ = 4 + (o * QT + t) % 4
                        bank = ps[bi_]
                        wts = list(dwsnap) + [(cV, fc_done)] + bank_free[bi_]
                        first = True
                        for fc_ in range(2):
                            for p_ in range(3):
                                lh = (whi, whi, wlo)[p_][par]
                                rhh = ((qhi, qlo)[fc_], (khi, klo)[fc_],
                                       (qhi, qlo)[fc_])[p_]
                                last = (p_ == 2 and fc_ == 1)
                                P.op("t", lambda e, lh=lh, rhh=rhh, fc_=fc_,
                                     tsl=tsl, bank=bank, f=first, last=last: mm(
                                         e, bank[:],
                                         lh[:, fc_ * 128:(fc_ + 1) * 128],
                                         rhh[:, tsl], f, last),
                                     waits=wts if first else (),
                                     inc=cT if last else None)
                                first = False
                                wts = ()
                        strip_user[par] = [(cT, cT.count)]
                        mmsnap = cT.count
                        if fb == 0:
                            P.op("v", lambda e, bank=bank, o=o, tsl=tsl:
                                 e.tensor_copy(Ur[:, o, tsl], bank[:]),
                                 waits=[(cT, mmsnap)], reads=(f"ps{bi_}",),
                                 writes=(f"Ur{o}",), inc=cV)
                        else:
                            P.op("v", lambda e, bank=bank, o=o, tsl=tsl:
                                 e.tensor_tensor(Ur[:, o, tsl], Ur[:, o, tsl],
                                                 bank[:], ALU.add),
                                 waits=[(cT, mmsnap)], reads=(f"ps{bi_}", f"Ur{o}"),
                                 writes=(f"Ur{o}",), inc=cV)
                        bank_free[bi_] = [(cV, cV.count)]
                    prefetch_strip()
                h_mm_snap[0] = cT.count
            xf_T_free[0] = cT.count
            snap_y = cV.count

            all_tiles = [(o, t4) for o in range(DC if PH >= 6 else 0)
                         for t4 in range(QT)]
            spk_bank_free = {j: list(bank_free[4 + j]) for j in range(4)}
            spk_mm_done = {}
            use_g = False
            g_tiles = [tl for i, tl in enumerate(all_tiles)
                       if use_g and i % 4 == 3]
            v_tiles = [tl for tl in all_tiles if tl not in g_tiles]

            def spike_chain(engine, pair, bufs, tags, first_waits, idn):
                mems, spks, _accs, mts = bufs
                mtag, stag, _atag, ttag = tags
                ys = [Ur[:, o, t4 * 512:(t4 + 1) * 512] for o, t4 in pair]
                fw = list(first_waits)
                pbank = [ps[4 + j] for j in range(len(pair))]
                acc_done = {}
                for j, (o, t4) in enumerate(pair):
                    P.op(engine, lambda e, y_=ys[j], sp=spks[j]: e.tensor_scalar(
                        sp[:], y_, 1.0, None, ALU.is_gt),
                        waits=fw + list(spk_mm_done.get(j, [])),
                        reads=(f"Ur{o}",), writes=(stag[j],), inc=cV)
                    fw = []
                    P.op("t", lambda e, sp=spks[j], pb=pbank[j]: mm(
                        e, pb[:], idn[:], sp[:], True, False),
                        waits=[(cV, cV.count), (cS, snap_identb)] +
                              spk_bank_free[j], inc=cT)
                    spk_mm_done[j] = [(cT, cT.count)]
                for st in range(2, 6):
                    for j, (o, t4) in enumerate(pair):
                        P.op(engine, lambda e, y_=ys[j], m=mems[j], mt=mts[j],
                             st=st: e.scalar_tensor_tensor(
                                 mt[:], (y_ if st == 2 else m[:]), 0.9,
                                 y_, ALU.mult, ALU.add),
                             reads=((f"Ur{o}",) if st == 2
                                    else (mtag[j], f"Ur{o}")),
                             writes=(ttag[j],))
                    for j, (o, t4) in enumerate(pair):
                        P.op(engine, lambda e, m=mems[j], mt=mts[j], sp=spks[j]:
                             e.tensor_tensor(m[:], mt[:], sp[:], ALU.subtract),
                             reads=(ttag[j], stag[j]), writes=(mtag[j],))
                    for j in range(len(pair)):
                        P.op(engine, lambda e, m=mems[j], sp=spks[j]:
                             e.tensor_scalar(sp[:], m[:], 1.0, None, ALU.is_gt),
                             waits=list(spk_mm_done.get(j, [])),
                             reads=(mtag[j],), writes=(stag[j],), inc=cV)
                        P.op("t", lambda e, sp=spks[j], pb=pbank[j], st=st: mm(
                            e, pb[:], idn[:], sp[:], False, st == 5),
                            waits=[(cV, cV.count)], inc=cT)
                        spk_mm_done[j] = [(cT, cT.count)]
                        if st == 5:
                            acc_done[j] = cT.count
                for j, (o, t4) in enumerate(pair):
                    P.op(engine, lambda e, mt=mts[j], pb=pbank[j], o=o, t4=t4:
                         e.scalar_tensor_tensor(
                             mt[:], pb[:], 0.2, xhi[:, o, t4 * 512:(t4 + 1) * 512],
                             ALU.mult, ALU.add),
                         waits=[(cT, acc_done[j])],
                         reads=(f"ps{4 + j}", f"xhi{o}"), writes=(ttag[j],))
                    P.op(engine, lambda e, y_=ys[j], mt=mts[j], o=o, t4=t4:
                         e.tensor_tensor(
                             y_, mt[:], xlo[:, o, t4 * 512:(t4 + 1) * 512],
                             ALU.add),
                         reads=(ttag[j], f"xlo{o}"), writes=(f"Ur{o}",),
                         inc=(cV if engine == "v" else cG))
                    spk_bank_free[j] = [(cV, cV.count)]
                return ys

            g_spk_done = cG.count
            if D >= 1024:
                tmh = (tm[0].ap()[:, 0:512], tm[0].ap()[:, 512:1024],
                       tm[1].ap()[:, 0:512], tm[1].ap()[:, 512:1024])
                if S >= 2048:
                    spk4 = tuple(qhi.ap()[:, i * 512:(i + 1) * 512]
                                 for i in range(4))
                    spk4_t = ("qhi0", "qhi1", "qhi2", "qhi3")
                    sp_idn = c_identb
                else:
                    spk4 = (bscr[0], bscr[1], bbuf[0], bbuf[1])
                    spk4_t = ("bscr0", "bscr1", "bbuf0", "bbuf1")
                    sp_idn = c_ident
                sp_bufs = ((rscr[0], rscr[1], xpc[0], xpc[1]),
                           spk4,
                           (eA[0], eA[1], eB[0], eB[1]), tmh)
                sp_tags = (("rscr0", "rscr1", "xpc0", "xpc1"),
                           spk4_t,
                           ("eA0", "eA1", "eB0", "eB1"),
                           ("tm0a", "tm0b", "tm1a", "tm1b"))
                W = 4
            else:
                if S >= 1024:
                    spk2 = tuple(qhi.ap()[:, i * 512:(i + 1) * 512]
                                 for i in range(2))
                    spk2_t = ("qhi0", "qhi1")
                    sp_idn = c_identb
                else:
                    spk2 = (bscr[0], bscr[1])
                    spk2_t = ("bscr0", "bscr1")
                    sp_idn = c_ident
                sp_bufs = ((rscr[0], rscr[1]), spk2,
                           (eA[0], eA[1]), (eB[0], eB[1]))
                sp_tags = (("rscr0", "rscr1"), spk2_t,
                           ("eA0", "eA1"), ("eB0", "eB1"))
                W = 2
            vw = [(cV, snap_y)]
            for i in range(0, len(v_tiles), W):
                spike_chain("v", v_tiles[i:i + W], sp_bufs, sp_tags, vw,
                            sp_idn)
                vw = []
            snap_res2 = cV.count
            xf_V_free[0] = snap_res2
            g_guard[0] = g_spk_done
            for j in range(4):
                if spk_bank_free[j]:
                    bank_free[4 + j] = spk_bank_free[j]

            for tb in range(TB if PH >= 7 else 0):
                par = tb % 2
                csl = slice(tb * 128, (tb + 1) * 128)
                ssnap = 0
                for c in range(DC):
                    bi_ = c % 4
                    bank = ps[bi_]
                    P.op("t", lambda e, bank=bank, c=c, csl=csl: e.transpose(
                        bank[:, 0:128], Ur[:, c, csl], c_ident[:]),
                        waits=[(cV, snap_res2), (cG, g_guard[0]),
                               (cDC, snap_consts)] + bank_free[bi_],
                        inc=cT)
                    if c % 2 == 0:
                        P.op("v", lambda e, bank=bank, c=c, par=par: e.tensor_copy(
                            tm[par][:, c * 128:(c + 1) * 128], bank[:, 0:128]),
                            waits=[(cT, cT.count)], reads=(f"ps{bi_}",),
                            writes=(f"tm{par}",), inc=cV)
                        bank_free[bi_] = [(cV, cV.count)]
                    else:
                        P.op("s", lambda e, bank=bank, c=c, par=par: e.copy(
                            tm[par][:, c * 128:(c + 1) * 128], bank[:, 0:128]),
                            waits=[(cT, cT.count)], reads=(f"ps{bi_}",),
                            writes=(f"tm{par}",), inc=cS)
                        ssnap = cS.count
                        bank_free[bi_] = [(cS, cS.count)]
                P.op("v", lambda e, par=par: e.tensor_reduce(
                    nsum[:], tm[par][:], AX.X, ALU.add, negate=True),
                    waits=[(cS, ssnap)] if ssnap else [],
                    reads=(f"tm{par}",), writes=("nsum",))
                P.op("v", lambda e: e.tensor_scalar_mul(negm[:], nsum[:], 1.0 / D),
                     reads=("nsum",), writes=("negm",), inc=cV)
                P.op("s", lambda e, par=par: e.activation(
                    tm[1 - par][:], tm[par][:], AF.Square, bias=negm[:],
                    accum_out=ssq[:]),
                    waits=[(cV, cV.count)], reads=(f"tm{par}", "negm"),
                    writes=(f"tm{1 - par}", "ssq"), inc=cS)
                P.op("v", lambda e: e.tensor_scalar(
                    vepsc[:], ssq[:], 1.0 / D, 1e-5, ALU.mult, ALU.add),
                    waits=[(cS, cS.count)], reads=("ssq",), writes=("vepsc",),
                    inc=cV)
                P.op("s", lambda e: e.activation(sqv[:], vepsc[:], AF.Sqrt),
                     waits=[(cV, cV.count)], reads=("vepsc",), writes=("sqv",),
                     inc=cS)
                P.op("v", lambda e: e.reciprocal(r2c[:], sqv[:]),
                     waits=[(cS, cS.count)], reads=("sqv",), writes=("r2c",))
                P.op("v", lambda e: e.tensor_scalar_mul(m2c[:], negm[:], -1.0),
                     reads=("negm",), writes=("m2c",), inc=cV)
                P.op("v", lambda e, par=par: e.tensor_scalar(
                    ob[par][:], tm[par][:], m2c[:], r2c[:], ALU.subtract, ALU.mult),
                    waits=[(cO[0], cO[0].count), (cO[1], cO[1].count),
                           (cG, g_guard[0])],
                    reads=(f"tm{par}", "m2c", "r2c"), writes=("ob0",), inc=cV)
                obsnap = cV.count
                P.op("g", lambda e, tb=tb, par=par, bcol=bcol: e.dma_start(
                    out=out[bcol + tb * 128: bcol + (tb + 1) * 128, :],
                    in_=ob[par][:]),
                    waits=[(cV, obsnap)], inc=cO[par], inc_amt=16)

        total_out = [cO[0].count, cO[1].count]
        with nc.Block() as block:
            @block.tensor
            def _(eng):
                P.emit("t", eng)

            @block.vector
            def _(eng):
                P.emit("v", eng)

            @block.scalar
            def _(eng):
                P.emit("s", eng)

            @block.sync
            def _(eng):
                P.emit("y", eng)

            @block.gpsimd
            def _(eng):
                P.emit("g", eng)
                if total_out[0]:
                    eng.wait_ge(cO[0].sem, total_out[0])
                if total_out[1]:
                    eng.wait_ge(cO[1].sem, total_out[1])

    return nc


def _mk_selh():
    s = np.zeros((33, 128), dtype=np.float32)
    s[0, 0:64] = 1.0
    s[32, 64:128] = 1.0
    return s


def kernel(L0_attn_x, L0_attn_c, L0_attn_Wq, L0_attn_bq, L0_attn_Wk, L0_attn_bk,
           L0_attn_Wv, L0_attn_bv, L0_attn_Wo, L0_attn_bo, L0_attn_rel_table,
           L0_ffn_W_fc, L0_ffn_W_proj):
    x = np.ascontiguousarray(np.asarray(L0_attn_x, dtype=np.float32))
    B, S, D = x.shape
    H = 16
    F = np.asarray(L0_ffn_W_fc).shape[0]
    NB = B // N_CORES
    cfg = make_cfg(S=S, D=D, H=H, F=F, NB=NB)

    rel = np.asarray(L0_attn_rel_table, np.float32)
    shared = dict(
        wq=pack_weight_strips_qkv(np.asarray(L0_attn_Wq, np.float32), D),
        wk=pack_weight_strips_qkv(np.asarray(L0_attn_Wk, np.float32), D),
        wv=pack_weight_strips_qkv(np.asarray(L0_attn_Wv, np.float32), D),
        wo=pack_weight_strips_qkv(np.asarray(L0_attn_Wo, np.float32), D),
        wfc=pack_wfc_strips(np.asarray(L0_ffn_W_fc, np.float32), D, F),
        wpr=pack_wpr_strips(np.asarray(L0_ffn_W_proj, np.float32), D, F),
        bqc=pack_bias_cols(np.asarray(L0_attn_bq, np.float32), D),
        bkc=pack_bias_cols(np.asarray(L0_attn_bk, np.float32), D),
        bvc=pack_bias_cols(np.asarray(L0_attn_bv, np.float32), D),
        boc=pack_bias_cols(np.asarray(L0_attn_bo, np.float32), D),
        crep=np.ascontiguousarray(
            np.repeat(rel[cfg["CLIP"]:cfg["CLIP"] + 1, :], 128, 0)),
        bands=pack_bands(rel, cfg),
        ident=np.eye(128, dtype=np.float32),
        selh=_mk_selh(),
    )
    in_maps = []
    for core in range(N_CORES):
        xc = x[core * NB:(core + 1) * NB]
        m = dict(shared)
        m["xT"] = np.ascontiguousarray(xc.transpose(2, 0, 1).reshape(D, NB * S))
        in_maps.append(m)

    nc = build_program(cfg)
    res = run_bass_kernel_spmd(nc, in_maps, core_ids=list(range(N_CORES)))
    outs = [np.asarray(r["out"]).reshape(NB, S, D) for r in res.results]
    return np.ascontiguousarray(np.concatenate(outs, axis=0))


# version 20 (final)
# speedup vs baseline: 1.1341x; 1.0132x over previous
import numpy as np
from contextlib import ExitStack

import concourse.bass as bass
import concourse.mybir as mybir
from concourse.bass_utils import run_bass_kernel_spmd

F32 = mybir.dt.float32
AF = mybir.ActivationFunctionType
ALU = mybir.AluOpType
AX = mybir.AxisListType

N_CORES = 8


def make_cfg(S=2048, D=1024, H=16, F=4096, NB=2):
    dk = D // H
    assert dk == 64 and D % 128 == 0 and S % 512 == 0 and F % 256 == 0 and H % 2 == 0
    return dict(
        S=S, D=D, H=H, F=F, NB=NB, dk=dk,
        DC=D // 128, QT=S // 512, KC=S // 128,
        FB=F // 256, TB=S // 128, HP=H // 2,
        REL=2 * 512 - 1, CLIP=2 * 512 - 2,
    )


def band_map(cfg):
    S, CLIP = cfg["S"], cfg["CLIP"]
    m = {}
    for qt in range(cfg["QT"]):
        for kc in range(cfg["KC"]):
            base = S - 1 + qt * 512 - kc * 128
            if base - 127 < CLIP:
                m[(kc, qt)] = base
    return m, sorted(set(m.values()))


def pack_weight_strips_qkv(W, D):
    DC = D // 128
    HPn = W.shape[1] // 128
    w = W.reshape(DC, 128, HPn, 128)
    return np.ascontiguousarray(w.transpose(2, 1, 0, 3))


def pack_wfc_strips(Wfc, D, F):
    DC = D // 128
    w = Wfc.reshape(F // 256, 2, 128, DC, 128)
    return np.ascontiguousarray(w.transpose(0, 1, 4, 3, 2))


def pack_wpr_strips(Wpr, D, F):
    DC = D // 128
    w = Wpr.reshape(DC, 128, F // 256, 2, 128)
    return np.ascontiguousarray(w.transpose(2, 0, 4, 3, 1))


def pack_bias_cols(b, D):
    return np.ascontiguousarray(b.reshape(D // 128, 128).T)


def pack_bands(rel, cfg):
    _, bases = band_map(cfg)
    CLIP = cfg["CLIP"]
    jj = np.arange(128)[:, None]
    ii = np.arange(512)[None, :]
    out = np.empty((cfg["H"], max(len(bases), 1), 128, 512), dtype=np.float32)
    for bi, base in enumerate(bases):
        idx = np.minimum(base + ii - jj, CLIP)
        out[:, bi] = rel[idx, :].transpose(2, 0, 1)
    return np.ascontiguousarray(out)


class Chan:
    __slots__ = ("name", "sem", "count")

    def __init__(self, name, sem):
        self.name, self.sem, self.count = name, sem, 0


class Prog:
    ENGINES = ("t", "v", "s", "g", "y")

    def __init__(self):
        self.ops = []
        self.wr = {e: set() for e in self.ENGINES}
        self.rd = {e: set() for e in self.ENGINES}
        self.n_drain = 0

    def op(self, eng, fn, waits=(), inc=None, inc_amt=1, reads=(), writes=()):
        for ch, val in waits:
            assert val <= ch.count, (ch.name, val, ch.count)
        if eng in ("v", "s", "g"):
            r, w = set(reads), set(writes)
            if (r | w) & self.wr[eng] or (w & self.rd[eng]):
                self.ops.append((eng, "drain", None))
                self.wr[eng].clear()
                self.rd[eng].clear()
                self.n_drain += 1
            self.rd[eng] |= r
            self.wr[eng] |= w
        self.ops.append((eng, "op", (fn, tuple(waits), inc, inc_amt)))
        if inc is not None:
            inc.count += inc_amt

    def emit(self, eng_key, eng):
        done = {}
        for e, kind, payload in self.ops:
            if e != eng_key:
                continue
            if kind == "drain":
                eng.drain()
                continue
            fn, waits, inc, inc_amt = payload
            for ch, val in waits:
                if val > 0 and done.get(id(ch), -1) < val:
                    eng.wait_ge(ch.sem, val)
                    done[id(ch)] = val
            instr = fn(eng)
            if inc is not None:
                instr.then_inc(inc.sem, inc_amt)


def build_program(cfg):
    S, D, H, F, NB = cfg["S"], cfg["D"], cfg["H"], cfg["F"], cfg["NB"]
    DC, QT, KC, FB, TB, HP = (cfg[k] for k in ("DC", "QT", "KC", "FB", "TB", "HP"))
    PH = cfg.get("ph", 99)
    bmap, bases = band_map(cfg)
    base_idx = {v: i for i, v in enumerate(bases)}
    NBASE = max(len(bases), 1)

    nc = bass.Bass("TRN2", target_bir_lowering=False, debug=False)

    xT = nc.dram_tensor("xT", [D, NB * S], F32, kind="ExternalInput").ap()
    wq = nc.dram_tensor("wq", [HP, 128, DC, 128], F32, kind="ExternalInput").ap()
    wk = nc.dram_tensor("wk", [HP, 128, DC, 128], F32, kind="ExternalInput").ap()
    wv = nc.dram_tensor("wv", [HP, 128, DC, 128], F32, kind="ExternalInput").ap()
    wo = nc.dram_tensor("wo", [DC, 128, DC, 128], F32, kind="ExternalInput").ap()
    wfc = nc.dram_tensor("wfc", [FB, 2, 128, DC, 128], F32, kind="ExternalInput").ap()
    wpr = nc.dram_tensor("wpr", [FB, DC, 128, 2, 128], F32, kind="ExternalInput").ap()
    bqc = nc.dram_tensor("bqc", [128, DC], F32, kind="ExternalInput").ap()
    bkc = nc.dram_tensor("bkc", [128, DC], F32, kind="ExternalInput").ap()
    bvc = nc.dram_tensor("bvc", [128, DC], F32, kind="ExternalInput").ap()
    boc = nc.dram_tensor("boc", [128, DC], F32, kind="ExternalInput").ap()
    crep = nc.dram_tensor("crep", [128, H], F32, kind="ExternalInput").ap()
    bands = nc.dram_tensor("bands", [H, NBASE, 128, 512], F32, kind="ExternalInput").ap()
    ident = nc.dram_tensor("ident", [128, 128], F32, kind="ExternalInput").ap()
    selh = nc.dram_tensor("selh", [33, 128], F32, kind="ExternalInput").ap()
    out = nc.dram_tensor("out", [NB * S, D], F32, kind="ExternalOutput").ap()

    with ExitStack() as ctx:
        def sb(name, shape, dt=F32):
            return ctx.enter_context(nc.sbuf_tensor(name, shape, dt))

        c_ident = sb("c_ident", [128, 128])
        c_identb = sb("c_identb", [128, 128], mybir.dt.bfloat16)
        selbc = sb("selbc", [33, 128])
        c_crep = sb("c_crep", [128, H])
        c_bq = sb("c_bq", [128, DC]); c_bk = sb("c_bk", [128, DC])
        c_bv = sb("c_bv", [128, DC]); c_bo = sb("c_bo", [128, DC])
        onescol = sb("onescol", [128, 1])
        onesrow = sb("onesrow", [1, 128])
        BF16 = mybir.dt.bfloat16
        xhi = sb("xhi", [128, DC, S], BF16)
        xlo = sb("xlo", [128, DC, S], BF16)
        xpc = [sb(f"xpc{i}", [128, 512]) for i in range(2)]
        wbf = [sb(f"wbf{i}", [128, DC * 128]) for i in range(2)]
        whi = [sb(f"whi{i}", [128, DC * 128], BF16) for i in range(2)]
        wlo = [sb(f"wlo{i}", [128, DC * 128], BF16) for i in range(2)]
        bbuf = [sb(f"bbuf{i}", [128, 512]) for i in range(2)]
        bscr = [sb(f"bscr{i}", [128, 512]) for i in range(2)]
        qhi = sb("qhi", [128, S], BF16); qlo = sb("qlo", [128, S], BF16)
        khi = sb("khi", [128, S], BF16); klo = sb("klo", [128, S], BF16)
        vtiles = sb("vtiles", [128, KC, 130])
        eA = [sb(f"eA{i}", [128, 512]) for i in range(2)]
        eB = [sb(f"eB{i}", [128, 512]) for i in range(2)]
        Ur = sb("Ur", [128, DC, S])
        rscr = [sb(f"rscr{i}", [128, 512]) for i in range(2)]
        tm = [sb(f"tm{i}", [128, D]) for i in range(2)]
        ob = [sb("ob0", [128, D])]
        ob = [ob[0], ob[0]]
        nsum = sb("nsum", [128, 1]); negm = sb("negm", [128, 1])
        m2c = sb("m2c", [128, 1]); r2c = sb("r2c", [128, 1])
        sqv = sb("sqv", [128, 1]); ssq = sb("ssq", [128, 1])
        vepsc = sb("vepsc", [128, 1])

        ps = [ctx.enter_context(nc.psum_tensor(f"ps{i}", [128, 512], F32))
              for i in range(8)]

        def chan(name):
            return Chan(name, ctx.enter_context(nc.semaphore(name)))

        cT = chan("cT"); cV = chan("cV"); cS = chan("cS"); cG = chan("cG")
        cDC = chan("cDC")
        cW = [chan("cW0"), chan("cW1")]
        cB = [chan("cB0"), chan("cB1")]
        cX = chan("cX"); cX2 = chan("cX2")
        cO = [chan("cO0"), chan("cO1")]

        P = Prog()

        def mm(e, o, l, r, st, sp, tp=None):
            return e.matmul(o, l, r, start=st, stop=sp, tile_position=tp)

        bank_free = {i: [] for i in range(8)}

        for dst, src in ((c_ident, ident), (c_crep, crep), (c_bq, bqc),
                         (c_bk, bkc), (c_bv, bvc), (c_bo, boc),
                         (selbc, selh)):
            P.op("y", lambda e, d=dst, s=src: e.dma_start(out=d[:], in_=s[:]),
                 inc=cDC, inc_amt=16)
        snap_consts = cDC.count
        P.op("s", lambda e: e.copy(c_identb[:], c_ident[:]),
             waits=[(cDC, snap_consts)], writes=("c_identb",), inc=cS)
        snap_identb = cS.count
        P.op("v", lambda e: e.memset(onesrow[:], 1.0), writes=("onesrow",))
        P.op("v", lambda e: e.memset(onescol[:], 1.0), writes=("onescol",))
        P.op("v", lambda e: e.memset(
            vtiles.ap().rearrange("p k (h f) -> p k h f", f=65)[:, :, :, 64:65], 1.0),
            inc=cV, writes=("vtiles",))

        strip_user = {0: [], 1: []}
        strip_split = {0: [], 1: []}
        sidx = [0]

        def load_strip(src_ap, ncols, split=False):
            par = sidx[0] % 2
            sidx[0] += 1
            P.op("y", lambda e: e.dma_start(out=wbf[par][:, 0:ncols], in_=src_ap),
                 waits=list(strip_user[par]) + list(strip_split[par]),
                 inc=cW[par], inc_amt=16)
            dwsnap = cW[par].count
            if split:
                P.op("s", lambda e: e.copy(whi[par][:, 0:ncols],
                                           wbf[par][:, 0:ncols]),
                     waits=[(cW[par], dwsnap)] + list(strip_user[par]),
                     reads=(f"wbf{par}",), writes=(f"whi{par}",), inc=cS)
                hisnap = cS.count
                P.op("v", lambda e: e.tensor_tensor(
                    wlo[par][:, 0:ncols], wbf[par][:, 0:ncols],
                    whi[par][:, 0:ncols], ALU.subtract),
                    waits=[(cS, hisnap)],
                    reads=(f"wbf{par}", f"whi{par}"), writes=(f"wlo{par}",),
                    inc=cV)
                strip_split[par] = [(cV, cV.count)]
                return par, [(cS, hisnap), (cV, cV.count)]
            strip_split[par] = []
            return par, [(cW[par], dwsnap)]

        xf_T_free = [0]
        xf_V_free = [0]
        g_guard = [0]
        xsplit = [0, 0]
        for b in range(NB):
            bcol = b * S

            xw0 = ([(cT, xf_T_free[0]), (cV, xf_V_free[0])] if b else [])
            cXp = [cX, cX2]
            for i_ in range(DC * QT):
                c, t4 = divmod(i_, QT)
                pi = i_ % 2
                tsl4 = slice(t4 * 512, (t4 + 1) * 512)
                P.op("y", lambda e, c=c, t4=t4, bcol=bcol, pi=pi: e.dma_start(
                    out=xpc[pi][:],
                    in_=xT[c * 128:(c + 1) * 128,
                          bcol + t4 * 512: bcol + (t4 + 1) * 512]),
                    waits=xw0 + ([(cV, xsplit[pi])] if xsplit[pi] else []),
                    inc=cXp[pi], inc_amt=16)
                xw0 = []
                P.op("s", lambda e, c=c, tsl4=tsl4, pi=pi: e.copy(
                    xhi[:, c, tsl4], xpc[pi][:]),
                    waits=[(cXp[pi], cXp[pi].count)] +
                          ([(cT, xf_T_free[0]), (cV, xf_V_free[0])]
                           if (b and i_ == 0) else []),
                    reads=(f"xpc{pi}",), writes=(f"xhi{c}",), inc=cS)
                P.op("v", lambda e, c=c, tsl4=tsl4, pi=pi: e.tensor_tensor(
                    xlo[:, c, tsl4], xpc[pi][:], xhi[:, c, tsl4], ALU.subtract),
                    waits=[(cS, cS.count)],
                    reads=(f"xpc{pi}", f"xhi{c}"), writes=(f"xlo{c}",), inc=cV)
                xsplit[pi] = cV.count
            snap_x = cV.count

            band_guard = [0]
            bscr_guard = [0, 0]
            pending_fin = [None]
            fin_done_t = {}

            def emit_finish():
                if pending_fin[0] is not None:
                    stages = pending_fin[0]
                    pending_fin[0] = None
                    for f in stages:
                        f()

            def emit_finish_first():
                if pending_fin[0] is not None and len(pending_fin[0]) == 2:
                    pending_fin[0][0]()
                    pending_fin[0] = pending_fin[0][1:]
            nHP = HP if PH >= 1 else 0
            if nHP:
                nxt_qk = (load_strip(wq[0].rearrange("p c f -> p (c f)"),
                                     DC * 128, split=True),
                          load_strip(wk[0].rearrange("p c f -> p (c f)"),
                                     DC * 128, split=True))
            for hp in range(nHP):
                hA, hB = 2 * hp, 2 * hp + 1
                pq, pk = nxt_qk
                emit_finish_first()

                for pj, ((par, dwsnap), dhi, dlo, bias_sb, dtag) in enumerate(
                        ((pq, qhi, qlo, c_bq, "q"), (pk, khi, klo, c_bk, "k"))):
                    if pj == 1:
                        pv_ = load_strip(wv[hp].rearrange("p c f -> p (c f)"),
                                         DC * 128, split=True)
                    for t in range(QT):
                        bi_ = t % 2
                        bank = ps[bi_]
                        wts = [(cV, snap_x)] + list(dwsnap) + bank_free[bi_]
                        first = True
                        for c in range(DC):
                            for p_ in range(3):
                                lh = (whi, whi, wlo)[p_][par]
                                rh = (xhi, xlo, xhi)[p_]
                                last = (p_ == 2 and c == DC - 1)
                                P.op("t", lambda e, lh=lh, rh=rh, c=c, t=t,
                                     bank=bank, f=first, last=last: mm(
                                         e, bank[:], lh[:, c * 128:(c + 1) * 128],
                                         rh[:, c, t * 512:(t + 1) * 512], f, last),
                                     waits=wts if first else (),
                                     inc=cT if last else None)
                                first = False
                                wts = ()
                        mmsnap = cT.count
                        hs = dhi[:, t * 512:(t + 1) * 512]
                        ls = dlo[:, t * 512:(t + 1) * 512]
                        P.op("v", lambda e, bank=bank, hs=hs, bb=bias_sb, hp=hp:
                             e.tensor_scalar_add(hs, bank[:], bb[:, hp:hp + 1]),
                             waits=[(cT, mmsnap), (cDC, snap_consts)],
                             reads=(f"ps{bi_}",), writes=(f"{dtag}hi{t}",))
                        P.op("v", lambda e, bank=bank, hs=hs, ls=ls, bb=bias_sb,
                             hp=hp: e.scalar_tensor_tensor(
                                 ls, bank[:], bb[:, hp:hp + 1], hs,
                                 ALU.add, ALU.subtract),
                             reads=(f"ps{bi_}", f"{dtag}hi{t}"),
                             writes=(f"{dtag}lo{t}",), inc=cV)
                        bank_free[bi_] = [(cV, cV.count)]
                        if pj == 0 and t == 0:
                            emit_finish()
                    strip_user[par] = [(cT, mmsnap)]

                par, dwsnap = pv_
                for t in range(QT):
                    bi_ = 4 + t % 2
                    bank = ps[bi_]
                    wts = [(cV, snap_x)] + list(dwsnap) + bank_free[bi_]
                    first = True
                    for c in range(DC):
                        for p_ in range(3):
                            lh = (whi, whi, wlo)[p_][par]
                            rh = (xhi, xlo, xhi)[p_]
                            last = (p_ == 2 and c == DC - 1)
                            P.op("t", lambda e, lh=lh, rh=rh, c=c, t=t, bank=bank,
                                 f=first, last=last: mm(
                                     e, bank[:], lh[:, c * 128:(c + 1) * 128],
                                     rh[:, c, t * 512:(t + 1) * 512], f, last),
                                 waits=wts if first else (),
                                 inc=cT if last else None)
                            first = False
                            wts = ()
                    mmsnap = cT.count
                    P.op("v", lambda e, bank=bank, hp=hp: e.tensor_scalar_add(
                        rscr[0][:], bank[:], c_bv[:, hp:hp + 1]),
                        waits=[(cT, mmsnap), (cDC, snap_consts)],
                        reads=(f"ps{bi_}",), writes=("rscr0",), inc=cV)
                    bank_free[bi_] = [(cV, cV.count)]
                    vsn = cV.count
                    for q4 in range(4):
                        kc = t * 4 + q4
                        ti_ = 6 + q4 % 2
                        tb_ = ps[ti_]
                        P.op("t", lambda e, tb_=tb_, q4=q4: e.transpose(
                            tb_[:, 0:128], rscr[0][:, q4 * 128:(q4 + 1) * 128],
                            c_ident[:]),
                            waits=[(cV, vsn), (cDC, snap_consts)] + bank_free[ti_],
                            inc=cT)
                        dst = vtiles.ap().rearrange(
                            "p k (h f) -> p k h f", f=65)[:, kc, :, 0:64]
                        P.op("v", lambda e, tb_=tb_, dst=dst: e.tensor_copy(
                            dst, tb_.ap().rearrange(
                                "p (h f) -> p h f", f=64)[:, 0:2, :]),
                            waits=[(cT, cT.count)], reads=(f"ps{ti_}",),
                            writes=("vtiles",), inc=cV)
                        bank_free[ti_] = [(cV, cV.count)]
                strip_user[par] = [(cT, mmsnap)]
                qkv_ready = cV.count
                if hp + 1 < nHP:
                    nxt_qk = (load_strip(wq[hp + 1].rearrange("p c f -> p (c f)"),
                                         DC * 128, split=True),
                              load_strip(wk[hp + 1].rearrange("p c f -> p (c f)"),
                                         DC * 128, split=True))

                for qt in range(QT if PH >= 2 else 0):
                    qsl = slice(qt * 512, (qt + 1) * 512)
                    s_snaps, exp_snaps, pv_snaps = {}, {}, {}

                    def issue_scores(kc, head=None):
                        par_ = kc % 2
                        heads = ((0, 0), (1, 64)) if head is None else \
                                (((0, 0),) if head == 0 else ((1, 64),))
                        wts0 = []
                        for h_i, r0 in heads:
                            wts0 += bank_free[2 * h_i + par_]
                        if kc < 2:
                            wts0 += [(cV, qkv_ready)]
                        for h_i, r0 in heads:
                            for p_ in range(3):
                                lhK = (khi, khi, klo)[p_]
                                rhQ = (qhi, qlo, qhi)[p_]
                                bi_ = 2 * h_i + par_
                                bank = ps[bi_]
                                P.op("t", lambda e, lhK=lhK, rhQ=rhQ, bank=bank,
                                     r0=r0, kc=kc, qsl=qsl, p_=p_: mm(
                                         e, bank[:],
                                         lhK[r0:r0 + 64, kc * 128:(kc + 1) * 128],
                                         rhQ[r0:r0 + 64, qsl], p_ == 0, p_ == 2,
                                         tp=(r0, 0)),
                                     waits=wts0,
                                     inc=cT if (p_ == 2 and
                                                h_i == heads[-1][0]) else None)
                                wts0 = ()
                        for h_i, _ in heads:
                            s_snaps[(kc, h_i)] = cT.count

                    def prefetch_band(kc):
                        if (kc, qt) not in bmap:
                            return
                        bi = base_idx[bmap[(kc, qt)]]
                        wdma = [(cV, band_guard[0])] if band_guard[0] else []
                        if g_guard[0]:
                            wdma.append((cG, g_guard[0]))
                        for h_i, h in enumerate((hA, hB)):
                            P.op("y", lambda e, h=h, bi=bi, h_i=h_i: e.dma_start(
                                out=bbuf[h_i][:], in_=bands[h, bi]),
                                waits=wdma, inc=cB[h_i], inc_amt=16)
                            wdma = []

                    def issue_exp(kc):
                        par_ = kc % 2
                        banded = (kc, qt) in bmap
                        guard = [(cT, pv_snaps[(kc - 2, 1)])] if kc >= 2 else []
                        for h_i, (h, ebuf) in enumerate(
                                ((hA, eA[par_]), (hB, eB[par_]))):
                            bi_ = 2 * h_i + par_
                            bank = ps[bi_]
                            etag = f"e{'AB'[h_i]}{par_}"
                            if banded:
                                P.op("v", lambda e, bank=bank, h_i=h_i:
                                     e.scalar_tensor_tensor(
                                         bscr[h_i][:], bank[:], 0.125, bbuf[h_i][:],
                                         ALU.mult, ALU.add),
                                     waits=[(cT, s_snaps[(kc, h_i)]),
                                            (cB[h_i], cB[h_i].count),
                                            (cS, bscr_guard[h_i])],
                                     reads=(f"ps{bi_}", f"bbuf{h_i}"),
                                     writes=(f"bscr{h_i}",), inc=cV)
                                band_guard[0] = cV.count
                                bank_free[bi_] = [(cV, cV.count)]
                                P.op("s", lambda e, ebuf=ebuf, h_i=h_i: e.activation(
                                    ebuf[:], bscr[h_i][:], AF.Exp),
                                    waits=[(cV, cV.count)] + guard,
                                    reads=(f"bscr{h_i}",), writes=(etag,), inc=cS)
                                bscr_guard[h_i] = cS.count
                            else:
                                P.op("s", lambda e, bank=bank, ebuf=ebuf, h=h:
                                     e.activation(ebuf[:], bank[:], AF.Exp,
                                                  bias=c_crep[:, h:h + 1],
                                                  scale=0.125),
                                     waits=[(cT, s_snaps[(kc, h_i)]),
                                            (cDC, snap_consts)] + guard,
                                     reads=(f"ps{bi_}",), writes=(etag,), inc=cS)
                                bank_free[bi_] = [(cS, cS.count)]
                            exp_snaps[(kc, h_i)] = cS.count

                    ub0 = 4 if qt % 2 == 0 else 6

                    def issue_pv(kc, h_i):
                        par_ = kc % 2
                        ub, ebuf = ((ps[ub0], eA[par_]),
                                    (ps[ub0 + 1], eB[par_]))[h_i]
                        hsl = slice(h_i * 65, h_i * 65 + 65)
                        wts = [(cS, exp_snaps[(kc, h_i)])]
                        if kc == 0:
                            wts += bank_free[ub0 + h_i]
                        P.op("t", lambda e, ub=ub, ebuf=ebuf, kc=kc, hsl=hsl: mm(
                            e, ub[0:65, :], vtiles[:, kc, hsl], ebuf[:],
                            kc == 0, kc == KC - 1),
                            waits=wts, inc=cT)
                        pv_snaps[(kc, h_i)] = cT.count

                    issue_scores(0)
                    prefetch_band(0)
                    for kc in range(KC):
                        if kc + 1 < KC:
                            issue_scores(kc + 1)
                        if kc == min(4, KC - 1):
                            emit_finish()
                        issue_exp(kc)
                        if kc + 1 < KC:
                            prefetch_band(kc + 1)
                        issue_pv(kc, 0)
                        issue_pv(kc, 1)

                    def make_finish(hp=hp, qsl=qsl, qt=qt, ub0=ub0,
                                    pv_last=None):
                        pv_last = pv_snaps[(KC - 1, 1)]
                        rs = [0]

                        def fin_a():
                            for h_i in range(2):
                                P.op("v", lambda e, ub=ps[ub0 + h_i], h_i=h_i:
                                     e.reciprocal(
                                         xpc[0][32 * h_i:32 * h_i + 1, :],
                                         ub[64:65, :]),
                                     waits=[(cT, pv_last)],
                                     reads=(f"ps{ub0 + h_i}",),
                                     writes=("xpc0",), inc=cV)
                            rs[0] = cV.count

                        def fin_b():
                            for h_i in range(2):
                                P.op("t", lambda e, ub=ps[ub0 + h_i], h_i=h_i:
                                     mm(e, ub[64:128, :],
                                        selbc[:, h_i * 64:(h_i + 1) * 64],
                                        xpc[0][0:33, :], True, True),
                                     waits=[(cV, rs[0]), (cDC, snap_consts)],
                                     inc=cT)
                            bcT = cT.count
                            for h_i in range(2):
                                bcs = rscr[h_i][0:64, :]
                                P.op("v", lambda e, ub=ps[ub0 + h_i], bcs=bcs:
                                     e.tensor_copy(bcs, ub[64:128, :]),
                                     waits=[(cT, bcT)],
                                     reads=(f"ps{ub0 + h_i}",),
                                     writes=(f"rscr{h_i}",))
                                dst = Ur[:, hp, qsl][h_i * 64:(h_i + 1) * 64, :]
                                P.op("v", lambda e, ub=ps[ub0 + h_i], bcs=bcs,
                                     dst=dst: e.tensor_tensor(
                                         dst, ub[0:64, :], bcs, ALU.mult),
                                     reads=(f"ps{ub0 + h_i}", f"rscr{h_i}"),
                                     writes=(f"Ur{hp}",), inc=cV)
                                bank_free[ub0 + h_i] = [(cV, cV.count)]
                            fin_done_t[qt] = cV.count
                        return fin_a, fin_b

                    pending_fin[0] = list(make_finish())
            emit_finish()
            snap_attn = cV.count

            res1_t = {}
            for t in range(QT if PH >= 3 else 0):
                tsl = slice(t * 512, (t + 1) * 512)
                lastgrp = 0
                for o in range(DC):
                    par, dwsnap = load_strip(
                        wo[o].rearrange("p c f -> p (c f)"), DC * 128)
                    wts = list(dwsnap) + \
                        [(cV, fin_done_t.get(t, snap_attn))] + bank_free[o]
                    for c in range(DC):
                        P.op("t", lambda e, c=c, o=o, par=par, tsl=tsl: mm(
                            e, ps[o][:], wbf[par][:, c * 128:(c + 1) * 128],
                            Ur[:, c, tsl], c == 0, c == DC - 1),
                            waits=wts if c == 0 else (),
                            inc=cT if c == DC - 1 else None)
                        wts = ()
                    strip_user[par] = [(cT, cT.count)]
                    lastgrp = cT.count
                for o in range(DC):
                    P.op("v", lambda e, o=o, tsl=tsl: e.scalar_tensor_tensor(
                        rscr[o % 2][:], ps[o][:], c_bo[:, o:o + 1], xhi[:, o, tsl],
                        ALU.add, ALU.add),
                        waits=[(cT, lastgrp), (cDC, snap_consts)],
                        reads=(f"ps{o}", f"xhi{o}"), writes=(f"rscr{o % 2}",))
                    P.op("v", lambda e, o=o, tsl=tsl: e.tensor_tensor(
                        Ur[:, o, tsl], rscr[o % 2][:], xlo[:, o, tsl], ALU.add),
                        reads=(f"rscr{o % 2}", f"xlo{o}"), writes=(f"Ur{o}",),
                        inc=cV)
                    bank_free[o] = [(cV, cV.count)]
                res1_t[t] = cV.count
            snap_res1 = cV.count

            nFB = FB if PH >= 5 else 0
            strip_plan = []
            for fb_ in range(nFB):
                strip_plan += [("fc", fb_, o_) for o_ in range(2)]
                strip_plan += [("pr", fb_, o_) for o_ in range(DC)]
            strip_idx = [0]

            def next_strip():
                kind, fb_, o_ = strip_plan[strip_idx[0]]
                strip_idx[0] += 1
                if kind == "fc":
                    return load_strip(wfc[fb_, o_].rearrange("p c f -> p (c f)"),
                                      DC * 128, split=True)
                return load_strip(wpr[fb_, o_].rearrange("p c f -> p (c f)"),
                                  256, split=True)

            pend = [next_strip() if strip_plan else None,
                    next_strip() if len(strip_plan) > 1 else None]

            def pop_strip():
                return pend.pop(0)

            def prefetch_strip():
                pend.append(next_strip()
                            if strip_idx[0] < len(strip_plan) else None)

            snap_x1_t = {}
            QTn = QT if PH >= 4 else 0
            sqk = [0]
            sqmm = {}
            for t in range(QTn):
                tsl = slice(t * 512, (t + 1) * 512)
                wts = [(cV, res1_t.get(t, snap_res1))] + bank_free[t]
                for c in range(DC):
                    P.op("t", lambda e, c=c, tsl=tsl, t=t: mm(
                        e, ps[t][0:1, :], onescol[:], Ur[:, c, tsl],
                        c == 0, c == DC - 1),
                        waits=wts if c == 0 else (),
                        inc=cT if c == DC - 1 else None)
                    wts = ()
                bw = list(bank_free[4 + t])
                for c in range(DC):
                    k = sqk[0]
                    sqk[0] += 1
                    w = [(cV, res1_t.get(t, snap_res1)), (cV, cV.count)]
                    if k >= 2:
                        w.append((cT, sqmm[k - 2]))
                    P.op("s", lambda e, c=c, tsl=tsl, k=k: e.activation(
                        rscr[k % 2][:], Ur[:, c, tsl], AF.Square),
                        waits=w, reads=(f"Ur{c}",), writes=(f"rscr{k % 2}",),
                        inc=cS)
                    P.op("t", lambda e, c=c, t=t, k=k: mm(
                        e, ps[4 + t][0:1, :], onescol[:], rscr[k % 2][:],
                        c == 0, c == DC - 1),
                        waits=[(cS, cS.count)] + bw, inc=cT)
                    bw = []
                    sqmm[k] = cT.count
            if D >= 1024:
                rrows = [tm[0].ap()[0:1, 0:512], tm[0].ap()[0:1, 512:1024],
                         tm[1].ap()[0:1, 0:512], tm[1].ap()[0:1, 512:1024]]
                mrrows = [ob[0].ap()[0:1, 0:512], ob[0].ap()[0:1, 512:1024],
                          xpc[0][0:1, 0:512], xpc[1][0:1, 0:512]]
                rtags = ["tm0a", "tm0b", "tm1a", "tm1b"]
                mtags = ["ob0a", "ob0b", "xpc0", "xpc1"]
            else:
                rrows = [bbuf[0][0:1, 0:512], bbuf[1][0:1, 0:512]] * 2
                mrrows = [xpc[0][0:1, 0:512], xpc[1][0:1, 0:512]] * 2
                rtags = ["bbuf0", "bbuf1"] * 2
                mtags = ["xpc0", "xpc1"] * 2
            rowsnap_t = {}
            for t in range(QTn):
                mrow = bscr[0][0:1, 0:512]; vrow = bscr[1][0:1, 0:512]
                msqr = eA[1][0:1, 0:512]; veps = eA[0][0:1, 0:512]
                sqvr = eB[0][0:1, 0:512]
                rrow, mrrow = rrows[t % 4], mrrows[t % 4]
                rtg, mtg = rtags[t % 4], mtags[t % 4]
                P.op("v", lambda e, mrow=mrow, t=t: e.tensor_scalar_mul(
                    mrow, ps[t][0:1, :], 1.0 / D),
                    waits=[(cT, sqmm[(t + 1) * DC - 1])] +
                          ([(cO[0], cO[0].count), (cO[1], cO[1].count)]
                           if t == 0 else []),
                    reads=(f"ps{t}",), writes=("bscr0",), inc=cV)
                bank_free[t] = [(cV, cV.count)]
                P.op("s", lambda e, mrow=mrow, msqr=msqr: e.activation(
                    msqr, mrow, AF.Square),
                    waits=[(cV, cV.count)], reads=("bscr0",), writes=("eA1",),
                    inc=cS)
                P.op("v", lambda e, vrow=vrow, msqr=msqr, t=t:
                     e.scalar_tensor_tensor(
                         vrow, ps[4 + t][0:1, :], 1.0 / D, msqr,
                         ALU.mult, ALU.subtract),
                     waits=[(cS, cS.count)], reads=(f"ps{4 + t}", "eA1"),
                     writes=("bscr1",), inc=cV)
                bank_free[4 + t] = [(cV, cV.count)]
                P.op("v", lambda e, vrow=vrow, veps=veps: e.tensor_scalar(
                    veps, vrow, 1e-5, None, ALU.add),
                    reads=("bscr1",), writes=("eA0",), inc=cV)
                P.op("s", lambda e, veps=veps, sqvr=sqvr: e.activation(
                    sqvr, veps, AF.Sqrt),
                    waits=[(cV, cV.count)], reads=("eA0",), writes=("eB0",),
                    inc=cS)
                P.op("v", lambda e, rrow=rrow, sqvr=sqvr: e.reciprocal(rrow, sqvr),
                     waits=[(cS, cS.count)], reads=("eB0",), writes=(rtg,))
                P.op("v", lambda e, mrow=mrow, rrow=rrow, mrrow=mrrow:
                     e.tensor_tensor(mrrow, mrow, rrow, ALU.mult),
                     reads=("bscr0", rtg), writes=(mtg,), inc=cV)
                rowsnap_t[t] = cV.count
            bc_t = {}
            for t in range(QTn):
                P.op("t", lambda e, t=t: mm(
                    e, ps[t][:], onesrow[:], rrows[t % 4], True, True),
                    waits=[(cV, rowsnap_t[t])] + bank_free[t], inc=cT)
                P.op("t", lambda e, t=t: mm(
                    e, ps[4 + t][:], onesrow[:], mrrows[t % 4], True, True),
                    waits=bank_free[4 + t], inc=cT)
                bc_t[t] = cT.count
            for t in range(QTn):
                tsl = slice(t * 512, (t + 1) * 512)
                for c in range(DC):
                    P.op("v", lambda e, c=c, tsl=tsl, t=t: e.tensor_tensor(
                        rscr[c % 2][:], Ur[:, c, tsl], ps[t][:], ALU.mult),
                        waits=[(cT, bc_t[t])], reads=(f"Ur{c}", f"ps{t}"),
                        writes=(f"rscr{c % 2}",))
                    P.op("v", lambda e, c=c, tsl=tsl, t=t: e.tensor_tensor(
                        bscr[c % 2][:], rscr[c % 2][:], ps[4 + t][:],
                        ALU.subtract),
                        reads=(f"rscr{c % 2}", f"ps{4 + t}"),
                        writes=(f"bscr{c % 2}",))
                    P.op("v", lambda e, c=c, tsl=tsl: e.tensor_copy(
                        xhi[:, c, tsl], bscr[c % 2][:]),
                        reads=(f"bscr{c % 2}",), writes=(f"xhi{c}",))
                    P.op("v", lambda e, c=c, tsl=tsl: e.tensor_tensor(
                        xlo[:, c, tsl], bscr[c % 2][:], xhi[:, c, tsl],
                        ALU.subtract),
                        reads=(f"bscr{c % 2}", f"xhi{c}"), writes=(f"xlo{c}",),
                        inc=cV)
                bank_free[t] = [(cV, cV.count)]
                bank_free[4 + t] = [(cV, cV.count)]
                snap_x1_t[t] = cV.count
            snap_x1 = cV.count

            h_mm_snap = [0]
            rscr_free = {0: [(cV, snap_x1)], 1: [(cV, snap_x1)]}
            for fb in range(nFB):
                for o in range(2):
                    par, dwsnap = pop_strip()
                    hh = (qhi, qlo)[o]
                    hl = (khi, klo)[o]
                    htg = ("qhi", "qlo")[o]
                    ltg = ("khi", "klo")[o]
                    for t in range(QT):
                        tsl = slice(t * 512, (t + 1) * 512)
                        bi_ = (o * QT + t) % 4
                        bank = ps[bi_]
                        wts = list(dwsnap) + \
                              [(cV, snap_x1_t.get(t, snap_x1))] + bank_free[bi_]
                        if h_mm_snap[0]:
                            wts.append((cT, h_mm_snap[0]))
                        first = True
                        for c in range(DC):
                            for p_ in range(3):
                                lh = (whi, whi, wlo)[p_][par]
                                rh = (xhi, xlo, xhi)[p_]
                                last = (p_ == 2 and c == DC - 1)
                                P.op("t", lambda e, lh=lh, rh=rh, c=c, tsl=tsl,
                                     bank=bank, f=first, last=last: mm(
                                         e, bank[:], lh[:, c * 128:(c + 1) * 128],
                                         rh[:, c, tsl], f, last),
                                     waits=wts if first else (),
                                     inc=cT if last else None)
                                first = False
                                wts = ()
                        strip_user[par] = [(cT, cT.count)]
                        mmsnap = cT.count
                        P.op("s", lambda e, bank=bank, o=o: e.activation(
                            rscr[o][:], bank[:], AF.Relu),
                            waits=[(cT, mmsnap)] + rscr_free[o],
                            reads=(f"ps{bi_}",), writes=(f"rscr{o}",))
                        P.op("s", lambda e, hh=hh, tsl=tsl, o=o: e.activation(
                            hh[:, tsl], rscr[o][:], AF.Square),
                            reads=(f"rscr{o}",), writes=(f"{htg}h{t}",), inc=cS)
                        P.op("v", lambda e, bank=bank, o=o: e.tensor_tensor(
                            bscr[o][:], rscr[o][:], bank[:], ALU.mult),
                            waits=[(cS, cS.count)], reads=(f"rscr{o}", f"ps{bi_}"),
                            writes=(f"bscr{o}",))
                        P.op("v", lambda e, hh=hh, hl=hl, tsl=tsl, o=o:
                             e.tensor_tensor(hl[:, tsl], bscr[o][:], hh[:, tsl],
                                             ALU.subtract),
                             reads=(f"bscr{o}", f"{htg}h{t}"),
                             writes=(f"{ltg}h{t}",), inc=cV)
                        rscr_free[o] = [(cV, cV.count)]
                        bank_free[bi_] = [(cV, cV.count)]
                    prefetch_strip()
                fc_done = cV.count
                for o in range(DC):
                    par, dwsnap = pop_strip()
                    for t in range(QT):
                        tsl = slice(t * 512, (t + 1) * 512)
                        bi_ = 4 + (o * QT + t) % 4
                        bank = ps[bi_]
                        wts = list(dwsnap) + [(cV, fc_done)] + bank_free[bi_]
                        first = True
                        for fc_ in range(2):
                            for p_ in range(3):
                                lh = (whi, whi, wlo)[p_][par]
                                rhh = ((qhi, qlo)[fc_], (khi, klo)[fc_],
                                       (qhi, qlo)[fc_])[p_]
                                last = (p_ == 2 and fc_ == 1)
                                P.op("t", lambda e, lh=lh, rhh=rhh, fc_=fc_,
                                     tsl=tsl, bank=bank, f=first, last=last: mm(
                                         e, bank[:],
                                         lh[:, fc_ * 128:(fc_ + 1) * 128],
                                         rhh[:, tsl], f, last),
                                     waits=wts if first else (),
                                     inc=cT if last else None)
                                first = False
                                wts = ()
                        strip_user[par] = [(cT, cT.count)]
                        mmsnap = cT.count
                        if fb == 0:
                            P.op("v", lambda e, bank=bank, o=o, tsl=tsl:
                                 e.tensor_copy(Ur[:, o, tsl], bank[:]),
                                 waits=[(cT, mmsnap)], reads=(f"ps{bi_}",),
                                 writes=(f"Ur{o}",), inc=cV)
                        else:
                            P.op("v", lambda e, bank=bank, o=o, tsl=tsl:
                                 e.tensor_tensor(Ur[:, o, tsl], Ur[:, o, tsl],
                                                 bank[:], ALU.add),
                                 waits=[(cT, mmsnap)], reads=(f"ps{bi_}", f"Ur{o}"),
                                 writes=(f"Ur{o}",), inc=cV)
                        bank_free[bi_] = [(cV, cV.count)]
                    prefetch_strip()
                h_mm_snap[0] = cT.count
            xf_T_free[0] = cT.count
            snap_y = cV.count

            all_tiles = [(o, t4) for o in range(DC if PH >= 6 else 0)
                         for t4 in range(QT)]
            spk_bank_free = {j: list(bank_free[4 + j]) for j in range(4)}
            spk_mm_done = {}
            use_g = False
            g_tiles = [tl for i, tl in enumerate(all_tiles)
                       if use_g and i % 4 == 3]
            v_tiles = [tl for tl in all_tiles if tl not in g_tiles]

            def spike_chain(engine, pair, bufs, tags, first_waits, idn):
                mems, spks, _accs, mts = bufs
                mtag, stag, _atag, ttag = tags
                ys = [Ur[:, o, t4 * 512:(t4 + 1) * 512] for o, t4 in pair]
                fw = list(first_waits)
                pbank = [ps[4 + j] for j in range(len(pair))]
                acc_done = {}
                for j, (o, t4) in enumerate(pair):
                    P.op(engine, lambda e, y_=ys[j], sp=spks[j]: e.tensor_scalar(
                        sp[:], y_, 1.0, None, ALU.is_gt),
                        waits=fw + list(spk_mm_done.get(j, [])),
                        reads=(f"Ur{o}",), writes=(stag[j],), inc=cV)
                    fw = []
                    P.op("t", lambda e, sp=spks[j], pb=pbank[j]: mm(
                        e, pb[:], idn[:], sp[:], True, False),
                        waits=[(cV, cV.count), (cS, snap_identb)] +
                              spk_bank_free[j], inc=cT)
                    spk_mm_done[j] = [(cT, cT.count)]
                for st in range(2, 6):
                    for j, (o, t4) in enumerate(pair):
                        P.op(engine, lambda e, y_=ys[j], m=mems[j], mt=mts[j],
                             st=st: e.scalar_tensor_tensor(
                                 mt[:], (y_ if st == 2 else m[:]), 0.9,
                                 y_, ALU.mult, ALU.add),
                             reads=((f"Ur{o}",) if st == 2
                                    else (mtag[j], f"Ur{o}")),
                             writes=(ttag[j],))
                    for j, (o, t4) in enumerate(pair):
                        P.op(engine, lambda e, m=mems[j], mt=mts[j], sp=spks[j]:
                             e.tensor_tensor(m[:], mt[:], sp[:], ALU.subtract),
                             reads=(ttag[j], stag[j]), writes=(mtag[j],))
                    for j in range(len(pair)):
                        P.op(engine, lambda e, m=mems[j], sp=spks[j]:
                             e.tensor_scalar(sp[:], m[:], 1.0, None, ALU.is_gt),
                             waits=list(spk_mm_done.get(j, [])),
                             reads=(mtag[j],), writes=(stag[j],), inc=cV)
                        P.op("t", lambda e, sp=spks[j], pb=pbank[j], st=st: mm(
                            e, pb[:], idn[:], sp[:], False, st == 5),
                            waits=[(cV, cV.count)], inc=cT)
                        spk_mm_done[j] = [(cT, cT.count)]
                        if st == 5:
                            acc_done[j] = cT.count
                for j, (o, t4) in enumerate(pair):
                    P.op(engine, lambda e, mt=mts[j], pb=pbank[j], o=o, t4=t4:
                         e.scalar_tensor_tensor(
                             mt[:], pb[:], 0.2, xhi[:, o, t4 * 512:(t4 + 1) * 512],
                             ALU.mult, ALU.add),
                         waits=[(cT, acc_done[j])],
                         reads=(f"ps{4 + j}", f"xhi{o}"), writes=(ttag[j],))
                    P.op(engine, lambda e, y_=ys[j], mt=mts[j], o=o, t4=t4:
                         e.tensor_tensor(
                             y_, mt[:], xlo[:, o, t4 * 512:(t4 + 1) * 512],
                             ALU.add),
                         reads=(ttag[j], f"xlo{o}"), writes=(f"Ur{o}",),
                         inc=(cV if engine == "v" else cG))
                    spk_bank_free[j] = [(cV, cV.count)]
                return ys

            g_spk_done = cG.count
            if D >= 1024:
                tmh = (tm[0].ap()[:, 0:512], tm[0].ap()[:, 512:1024],
                       tm[1].ap()[:, 0:512], tm[1].ap()[:, 512:1024])
                if S >= 2048:
                    spk4 = tuple(qhi.ap()[:, i * 512:(i + 1) * 512]
                                 for i in range(4))
                    spk4_t = ("qhi0", "qhi1", "qhi2", "qhi3")
                    sp_idn = c_identb
                else:
                    spk4 = (bscr[0], bscr[1], bbuf[0], bbuf[1])
                    spk4_t = ("bscr0", "bscr1", "bbuf0", "bbuf1")
                    sp_idn = c_ident
                sp_bufs = ((rscr[0], rscr[1], xpc[0], xpc[1]),
                           spk4,
                           (eA[0], eA[1], eB[0], eB[1]), tmh)
                sp_tags = (("rscr0", "rscr1", "xpc0", "xpc1"),
                           spk4_t,
                           ("eA0", "eA1", "eB0", "eB1"),
                           ("tm0a", "tm0b", "tm1a", "tm1b"))
                W = 4
            else:
                if S >= 1024:
                    spk2 = tuple(qhi.ap()[:, i * 512:(i + 1) * 512]
                                 for i in range(2))
                    spk2_t = ("qhi0", "qhi1")
                    sp_idn = c_identb
                else:
                    spk2 = (bscr[0], bscr[1])
                    spk2_t = ("bscr0", "bscr1")
                    sp_idn = c_ident
                sp_bufs = ((rscr[0], rscr[1]), spk2,
                           (eA[0], eA[1]), (eB[0], eB[1]))
                sp_tags = (("rscr0", "rscr1"), spk2_t,
                           ("eA0", "eA1"), ("eB0", "eB1"))
                W = 2
            vw = [(cV, snap_y)]
            for i in range(0, len(v_tiles), W):
                spike_chain("v", v_tiles[i:i + W], sp_bufs, sp_tags, vw,
                            sp_idn)
                vw = []
            snap_res2 = cV.count
            xf_V_free[0] = snap_res2
            g_guard[0] = g_spk_done
            for j in range(4):
                if spk_bank_free[j]:
                    bank_free[4 + j] = spk_bank_free[j]

            for tb in range(TB if PH >= 7 else 0):
                par = tb % 2
                csl = slice(tb * 128, (tb + 1) * 128)
                ssnap = 0
                for c in range(DC):
                    bi_ = c % 4
                    bank = ps[bi_]
                    P.op("t", lambda e, bank=bank, c=c, csl=csl: e.transpose(
                        bank[:, 0:128], Ur[:, c, csl], c_ident[:]),
                        waits=[(cV, snap_res2), (cG, g_guard[0]),
                               (cDC, snap_consts)] + bank_free[bi_],
                        inc=cT)
                    if c % 2 == 0:
                        P.op("v", lambda e, bank=bank, c=c, par=par: e.tensor_copy(
                            tm[par][:, c * 128:(c + 1) * 128], bank[:, 0:128]),
                            waits=[(cT, cT.count)], reads=(f"ps{bi_}",),
                            writes=(f"tm{par}",), inc=cV)
                        bank_free[bi_] = [(cV, cV.count)]
                    else:
                        P.op("s", lambda e, bank=bank, c=c, par=par: e.copy(
                            tm[par][:, c * 128:(c + 1) * 128], bank[:, 0:128]),
                            waits=[(cT, cT.count)], reads=(f"ps{bi_}",),
                            writes=(f"tm{par}",), inc=cS)
                        ssnap = cS.count
                        bank_free[bi_] = [(cS, cS.count)]
                P.op("v", lambda e, par=par: e.tensor_reduce(
                    nsum[:], tm[par][:], AX.X, ALU.add, negate=True),
                    waits=[(cS, ssnap)] if ssnap else [],
                    reads=(f"tm{par}",), writes=("nsum",))
                P.op("v", lambda e: e.tensor_scalar_mul(negm[:], nsum[:], 1.0 / D),
                     reads=("nsum",), writes=("negm",), inc=cV)
                P.op("s", lambda e, par=par: e.activation(
                    tm[1 - par][:], tm[par][:], AF.Square, bias=negm[:],
                    accum_out=ssq[:]),
                    waits=[(cV, cV.count)], reads=(f"tm{par}", "negm"),
                    writes=(f"tm{1 - par}", "ssq"), inc=cS)
                P.op("v", lambda e: e.tensor_scalar(
                    vepsc[:], ssq[:], 1.0 / D, 1e-5, ALU.mult, ALU.add),
                    waits=[(cS, cS.count)], reads=("ssq",), writes=("vepsc",),
                    inc=cV)
                P.op("s", lambda e: e.activation(sqv[:], vepsc[:], AF.Sqrt),
                     waits=[(cV, cV.count)], reads=("vepsc",), writes=("sqv",),
                     inc=cS)
                P.op("v", lambda e: e.reciprocal(r2c[:], sqv[:]),
                     waits=[(cS, cS.count)], reads=("sqv",), writes=("r2c",))
                P.op("v", lambda e: e.tensor_scalar_mul(m2c[:], negm[:], -1.0),
                     reads=("negm",), writes=("m2c",), inc=cV)
                P.op("v", lambda e, par=par: e.tensor_scalar(
                    ob[par][:], tm[par][:], m2c[:], r2c[:], ALU.subtract, ALU.mult),
                    waits=[(cO[0], cO[0].count), (cO[1], cO[1].count),
                           (cG, g_guard[0])],
                    reads=(f"tm{par}", "m2c", "r2c"), writes=("ob0",), inc=cV)
                obsnap = cV.count
                P.op("g", lambda e, tb=tb, par=par, bcol=bcol: e.dma_start(
                    out=out[bcol + tb * 128: bcol + (tb + 1) * 128, :],
                    in_=ob[par][:]),
                    waits=[(cV, obsnap)], inc=cO[par], inc_amt=16)

        total_out = [cO[0].count, cO[1].count]
        with nc.Block() as block:
            @block.tensor
            def _(eng):
                P.emit("t", eng)

            @block.vector
            def _(eng):
                P.emit("v", eng)

            @block.scalar
            def _(eng):
                P.emit("s", eng)

            @block.sync
            def _(eng):
                P.emit("y", eng)

            @block.gpsimd
            def _(eng):
                P.emit("g", eng)
                if total_out[0]:
                    eng.wait_ge(cO[0].sem, total_out[0])
                if total_out[1]:
                    eng.wait_ge(cO[1].sem, total_out[1])

    return nc


def _mk_selh():
    s = np.zeros((33, 128), dtype=np.float32)
    s[0, 0:64] = 1.0
    s[32, 64:128] = 1.0
    return s


def kernel(L0_attn_x, L0_attn_c, L0_attn_Wq, L0_attn_bq, L0_attn_Wk, L0_attn_bk,
           L0_attn_Wv, L0_attn_bv, L0_attn_Wo, L0_attn_bo, L0_attn_rel_table,
           L0_ffn_W_fc, L0_ffn_W_proj):
    x = np.ascontiguousarray(np.asarray(L0_attn_x, dtype=np.float32))
    B, S, D = x.shape
    H = 16
    F = np.asarray(L0_ffn_W_fc).shape[0]
    NB = B // N_CORES
    cfg = make_cfg(S=S, D=D, H=H, F=F, NB=NB)

    rel = np.asarray(L0_attn_rel_table, np.float32)
    shared = dict(
        wq=pack_weight_strips_qkv(np.asarray(L0_attn_Wq, np.float32), D),
        wk=pack_weight_strips_qkv(np.asarray(L0_attn_Wk, np.float32), D),
        wv=pack_weight_strips_qkv(np.asarray(L0_attn_Wv, np.float32), D),
        wo=pack_weight_strips_qkv(np.asarray(L0_attn_Wo, np.float32), D),
        wfc=pack_wfc_strips(np.asarray(L0_ffn_W_fc, np.float32), D, F),
        wpr=pack_wpr_strips(np.asarray(L0_ffn_W_proj, np.float32), D, F),
        bqc=pack_bias_cols(np.asarray(L0_attn_bq, np.float32), D),
        bkc=pack_bias_cols(np.asarray(L0_attn_bk, np.float32), D),
        bvc=pack_bias_cols(np.asarray(L0_attn_bv, np.float32), D),
        boc=pack_bias_cols(np.asarray(L0_attn_bo, np.float32), D),
        crep=np.ascontiguousarray(
            np.repeat(rel[cfg["CLIP"]:cfg["CLIP"] + 1, :], 128, 0)),
        bands=pack_bands(rel, cfg),
        ident=np.eye(128, dtype=np.float32),
        selh=_mk_selh(),
    )
    in_maps = []
    for core in range(N_CORES):
        xc = x[core * NB:(core + 1) * NB]
        m = dict(shared)
        m["xT"] = np.ascontiguousarray(xc.transpose(2, 0, 1).reshape(D, NB * S))
        in_maps.append(m)

    nc = build_program(cfg)
    res = run_bass_kernel_spmd(nc, in_maps, core_ids=list(range(N_CORES)))
    outs = [np.asarray(r["out"]).reshape(NB, S, D) for r in res.results]
    return np.ascontiguousarray(np.concatenate(outs, axis=0))
